# version 03
# baseline (speedup 1.0000x reference)
import numpy as np
import concourse.bass as bass
import concourse.mybir as mybir
from concourse.bass import IndirectOffsetOnAxis
from concourse.bass_utils import run_bass_kernel_spmd

F32 = mybir.dt.float32
BF16 = mybir.dt.bfloat16
I32 = mybir.dt.int32
U32 = mybir.dt.uint32
U8 = mybir.dt.uint8
AF = mybir.ActivationFunctionType
OP = mybir.AluOpType
AX = mybir.AxisListType

NEG = -3.0e38
EXP_OFF = -4.0
LN_EPS = 1e-5
RMS_EPS = 1.1920929e-07


def full_cfg():
    return dict(D=1024, HATT=16, HFFN=8, NS=256, NEXP=65536, NT_R=16, taps=())


class Sem:
    def __init__(self, nc, name):
        self.h = nc.alloc_semaphore(name)
        self.n = 0

    def inc(self, k=1):
        self.n += k
        return self.n


def build(cfg):
    D = cfg["D"]; NCH = D // 128
    HATT = cfg["HATT"]; DH = 64
    HFFN = cfg["HFFN"]; NS = cfg["NS"]; NEXP = cfg["NEXP"]
    NT_R = cfg["NT_R"]; R = NT_R * 128
    NT_E = 2 * NT_R; E = 2 * R
    NQS = NT_R // 4
    DI = HATT * DH
    NDI = DI // 128
    KV = 2 * DI
    FQ = HFFN * 256
    NFC = FQ // 128
    NPAIR = HATT // 2
    NSH = NS // 128
    DSPL = [(i, i * 512, min(512, D - i * 512)) for i in range((D + 511) // 512)]
    taps = set(cfg.get("taps", ()))

    nc = bass.Bass("TRN2", target_bir_lowering=False, detect_race_conditions=False)

    xe = nc.dram_tensor("xe", [E, D], F32, kind="ExternalInput").ap()
    cflag = nc.dram_tensor("cflag", [128, 1], F32, kind="ExternalInput").ap()
    nw_in = nc.dram_tensor("nw", [D], F32, kind="ExternalInput").ap()
    wq_in = nc.dram_tensor("wq", [D, DI], F32, kind="ExternalInput").ap()
    wkv_in = nc.dram_tensor("wkv", [D, KV], F32, kind="ExternalInput").ap()
    wout_in = nc.dram_tensor("wout", [DI, D], F32, kind="ExternalInput").ap()
    wg_in = nc.dram_tensor("wg", [D, HATT], F32, kind="ExternalInput").ap()
    wf_in = nc.dram_tensor("wf", [D, FQ], F32, kind="ExternalInput").ap()
    bq_in = nc.dram_tensor("bq", [FQ], F32, kind="ExternalInput").ap()
    k1_in = nc.dram_tensor("k1", [NS, 128], F32, kind="ExternalInput").ap()
    k2_in = nc.dram_tensor("k2", [NS, 128], F32, kind="ExternalInput").ap()
    wdn_in = nc.dram_tensor("wdn", [NEXP, D], F32, kind="ExternalInput").ap()
    wup_in = nc.dram_tensor("wup", [NEXP, D], F32, kind="ExternalInput").ap()
    out_d = nc.dram_tensor("out", [R, D], F32, kind="ExternalOutput").ap()
    tap_d = {}
    def mktap(name, cols, dtype=F32):
        if name in taps:
            tap_d[name] = nc.dram_tensor("t_" + name, [128, cols], dtype,
                                         kind="ExternalOutput").ap()
    mktap("xn2fem", NCH * E, BF16); mktap("rinv", NT_E); mktap("ffnq", 2 * 512, BF16)
    mktap("s12", 2 * NS); mktap("ei", NT_R * HFFN * 10, I32)
    mktap("sc", NT_R * HFFN * 10); mktap("gff", NT_R * D, BF16)
    mktap("gates", R, BF16); mktap("gatedT", NPAIR * R, BF16)
    mktap("kpair", E, BF16)

    def dotap(name, src_ap, wait_sem, wait_val):
        if name not in tap_d:
            return
        tv = s_st.inc(16)
        def f(e, src_ap=src_ap, wait_sem=wait_sem, wait_val=wait_val, name=name):
            e.wait_ge(wait_sem.h, wait_val)
            e.dma_start(tap_d[name], src_ap).then_inc(s_st.h, 16)
        sy(f)
        s_v.inc()
        def f(v, tv=tv):
            v.wait_ge(s_st.h, tv)
            v.engine_nop()
            v.drain().then_inc(s_v.h, 1)
        ve(f)

    A = nc.alloc_sbuf_tensor
    xn2fem = A("xn2fem", [128, NCH * E], BF16).ap()
    g_out = A("g_out", [128, NT_R * D], BF16).ap()
    gatedT = A("gatedT", [128, NPAIR * R], BF16).ap()
    ei_all = A("ei_all", [128, NT_R * HFFN * 10], I32).ap()
    sc_all = A("sc_all", [128, NT_R * HFFN * 10], F32).ap()
    gstg = A("gstg", [128, R], BF16).ap()
    rep_c1 = A("rep_c1", [128, HFFN * 2 * NS], BF16).ap()
    wg_sb = A("wg_sb", [128, NCH * HATT], BF16).ap()
    k1t = A("k1t", [128, NS], BF16).ap()
    k2t = A("k2t", [128, NS], BF16).ap()
    ident = A("ident", [128, 128], BF16).ap()
    ones_bf = A("ones_bf", [128, 128], BF16).ap()
    onescol = A("onescol", [128, 1], BF16).ap()
    tril = A("tril", [128, 128], BF16).ap()
    iota10 = A("iota10", [128, 100], F32).ap()
    cfl = A("cfl", [128, 1], F32).ap()
    nw_f = A("nw_f", [128, NCH], F32).ap()
    bq_f = A("bq_f", [128, 2 * HFFN], F32).ap()
    bq_bf = A("bq_bf", [128, 2 * HFFN], BF16).ap()
    c12 = A("c12", [128, 2 * NS], F32).ap()
    rinv = A("rinv", [128, NT_E], F32).ap()
    stat = A("stat", [128, 16], F32).ap()
    cexp = A("cexp", [128, 1], F32).ap()
    wld = A("wld", [128, 2 * 512], F32).ap()
    topv = A("topv", [128, 32], F32).ap()
    topi = A("topi", [128, 32], U32).ap()
    comb = A("comb", [128, 160], F32).ap()
    crep = A("crep", [128, 160], F32).ap()
    cv8 = A("cv8", [128, 16], F32).ap()
    ci8 = A("ci8", [128, 16], U32).ap()
    bji = A("bji", [128, 16], U32).ap()
    bjf = A("bjf", [128, 32], F32).ap()
    i1f = A("i1f", [128, 32], F32).ap()
    mskt = A("mskt", [128, 200], F32).ap()
    e12 = A("e12", [128, 20], F32).ap()
    eff = A("eff", [128, 16], F32).ap()
    valst = A("valst", [128, 16], F32).ap()
    pval = A("pval", [128, 16], F32).ap()
    hcol = A("hcol", [128, 16], F32).ap()
    hgel = A("hgel", [128, 16], F32).ap()
    diag = A("diag", [128, 2 * 128], BF16).ap()
    dmpA = A("dmpA", [128, D], BF16).ap()

    AR = 40 * 1024
    arena = A("arena", [128, AR], U8).ap()

    def carve(off, n, dtype):
        sz = 4 if dtype in (F32, I32, U32) else 2
        assert off + n * sz <= AR, (off, n)
        return arena[:, off:off + n * sz].bitcast(dtype)

    wfp = carve(0, NCH * 256, BF16)
    ffnq = carve(NCH * 256 * 2, 2 * 512, BF16)
    _o2 = NCH * 256 * 2 + 2 * 512 * 2
    s12 = carve(_o2, 2 * NS, F32)
    srep = carve(_o2 + 2 * NS * 4, 2 * NS, F32)
    assert _o2 + 4 * NS * 4 <= 28 * 1024

    NGP = 5
    gbuf = carve(0, 2 * NGP * D, BF16)
    dmp3 = carve(2 * NGP * D * 2, D, BF16)
    assert 2 * NGP * D * 2 + D * 2 <= 28 * 1024
    xn_tok = carve(28 * 1024, 2 * D, BF16)

    _o = 0
    wkvq_a = carve(_o, NCH * 128, BF16); _o += NCH * 128 * 2
    kpair = carve(_o, E, BF16); _o += E * 2
    v2 = carve(_o, 2 * NT_E * 64, BF16); _o += 2 * NT_E * 64 * 2
    qpair = carve(_o, R, BF16); _o += R * 2
    NPT = 3
    pt = carve(_o, NPT * 512, BF16); _o += NPT * 512 * 2
    lrow = carve(_o, 512, F32); _o += 512 * 4
    glrow = carve(_o, 512, BF16)
    fin = carve(_o, 512, BF16); _o += 512 * 2
    assert _o <= 28 * 1024, _o
    glstg = carve(28 * 1024, 2 * 512, BF16)

    wout_a = carve(0, NDI * D, BF16)
    otmp2 = carve(NDI * D * 2, 2 * D, F32)
    assert NDI * D * 2 + 2 * D * 4 <= 28 * 1024

    xload = carve(AR - 2 * D * 4, 2 * D, F32)
    assert 32 * 1024 <= AR - 2 * D * 4

    PS = [nc.alloc_psum_tensor(f"ps{i}", [128, 512], F32).ap() for i in range(8)]
    PSB = [p.bitcast(BF16) for p in PS]

    names = ["s_x", "s_w", "s_gp", "s_c", "s_v", "s_s", "s_t", "s_gd", "s_gc",
             "s_um", "s_dg", "s_pt", "s_av", "s_fz", "s_sg", "s_st"]
    SM = {n: Sem(nc, n) for n in names}
    (s_x, s_w, s_gp, s_c, s_v, s_s, s_t, s_gd, s_gc, s_um, s_dg, s_pt, s_av,
     s_fz, s_sg, s_st) = [SM[n] for n in names]
    s_xp = [Sem(nc, f"s_xp{i}") for i in range(2)]
    s_wp = [Sem(nc, f"s_wp{i}") for i in range(2)]
    s_sgp = [Sem(nc, f"s_sgp{i}") for i in range(2)]
    s_stp = [Sem(nc, f"s_stp{i}") for i in range(2)]
    s_gdn = [Sem(nc, f"s_gdn{i}") for i in range(NGP)]
    s_gup = [Sem(nc, f"s_gup{i}") for i in range(NGP)]

    BK = {}
    blkref = {}
    def sy(f): blkref["b"].sync(f)
    def gp(f): blkref["b"].gpsimd(f)
    def ve(f): blkref["b"].vector(f)
    def sa(f): blkref["b"].scalar(f)
    def te(f): blkref["b"].tensor(f)

    def _p0fin():
        BK["rep_c1_c"] = s_c.n
        BK["p0_c"] = s_c.n
        BK["p0_t"] = s_t.n

    def p0():
        def f(g):
            g.iota(iota10.rearrange("p (k j) -> p k j", j=10), [[0, 10], [1, 10]],
                   channel_multiplier=0, allow_small_or_imprecise_dtypes=True)
            g.iota(mskt[:, 0:128], [[1, 128]], channel_multiplier=-1,
                   allow_small_or_imprecise_dtypes=True).then_inc(s_gp.h, s_gp.inc())
        gp(f)

        nx = s_x.inc(48)
        def f(e):
            e.dma_start(cfl, cflag).then_inc(s_x.h, 16)
            e.dma_start(nw_f, nw_in.rearrange("(c p) -> p c", p=128),
                        allow_slow_non_contiguous=True).then_inc(s_x.h, 16)
            e.dma_start(bq_f, bq_in.rearrange("(hh d) -> d hh", d=128),
                        allow_slow_non_contiguous=True).then_inc(s_x.h, 16)
        sy(f)

        ngp, nxc = s_gp.n, nx
        ncn = s_c.inc()
        def f(v):
            v.wait_ge(s_gp.h, ngp)
            v.wait_ge(s_x.h, nxc)
            v.tensor_scalar(out=ident, in0=mskt[:, 0:128], scalar1=0.0, scalar2=None,
                            op0=OP.is_equal)
            v.tensor_scalar(out=tril, in0=mskt[:, 0:128], scalar1=0.0, scalar2=None,
                            op0=OP.is_ge)
            v.memset(ones_bf, 1.0)
            v.memset(onescol, 1.0)
            v.memset(cexp, EXP_OFF)
            v.memset(gstg, 0.0)
            v.memset(sc_all, 0.0)
            v.memset(hcol, 0.0)
            v.tensor_copy(bq_bf, bq_f)
            v.engine_nop()
            v.drain().then_inc(s_c.h, 1)
        ve(f)
        BK["consts"] = ncn
        if cfg.get('p0cut') == 1: return _p0fin()

        prev_cast = [ncn, ncn]
        piece = 0
        for ch in range(NCH):
            buf = piece % 2
            st_ap = wld[:, buf * 512:buf * 512 + HATT]
            need = prev_cast[buf]
            sw = s_wp[buf]
            ndma = sw.inc(16)
            def f(e, st_ap=st_ap, ch=ch, need=need, sw=sw):
                e.wait_ge(s_c.h, need)
                e.dma_start(st_ap, wg_in[ch * 128:(ch + 1) * 128, :]).then_inc(sw.h, 16)
            sy(f)
            ncst = s_c.inc()
            def f(v, st_ap=st_ap, ch=ch, ndma=ndma, sw=sw):
                v.wait_ge(sw.h, ndma)
                v.tensor_scalar(out=wg_sb[:, ch * HATT:(ch + 1) * HATT], in0=st_ap,
                                scalar1=nw_f[:, ch:ch + 1], scalar2=None, op0=OP.mult)
                v.drain().then_inc(s_c.h, 1)
            ve(f)
            prev_cast[buf] = ncst
            piece += 1
        if cfg.get('p0cut') == 2: return _p0fin()

        for (k_in, k_t) in ((k1_in, k1t), (k2_in, k2t)):
            for half in range(NSH):
                buf = piece % 2
                st_ap = wld[:, buf * 512:buf * 512 + 128]
                need = prev_cast[buf]
                sw = s_wp[buf]
                ndma = sw.inc(16)
                def f(e, st_ap=st_ap, k_in=k_in, half=half, need=need, sw=sw):
                    e.wait_ge(s_c.h, need)
                    e.dma_start(st_ap, k_in[half * 128:half * 128 + 128, :]) \
                     .then_inc(sw.h, 16)
                sy(f)
                ncst = s_c.inc()
                def f(v, st_ap=st_ap, half=half, ndma=ndma, sw=sw):
                    v.wait_ge(sw.h, ndma)
                    v.tensor_copy(dmpA[:, half * 128:half * 128 + 128], st_ap)
                    v.drain().then_inc(s_c.h, 1)
                ve(f)
                prev_cast[buf] = ncst
                piece += 1
            ncnow = s_c.n
            ntp = s_t.inc()
            def f(t, ncnow=ncnow):
                t.wait_ge(s_c.h, ncnow)
                mm = None
                for half in range(NSH):
                    mm = t.transpose(PSB[0][:, half * 128:half * 128 + 128],
                                     dmpA[:, half * 128:half * 128 + 128], ident)
                mm.then_inc(s_t.h, 1)
            te(f)
            ncst = s_c.inc()
            def f(v, k_t=k_t, ntp=ntp):
                v.wait_ge(s_t.h, ntp)
                v.tensor_copy(k_t, PSB[0][:, 0:NS])
                v.drain().then_inc(s_c.h, 1)
            ve(f)
            prev_cast[0] = prev_cast[1] = ncst
        if cfg.get('p0cut') == 3: return _p0fin()

        c12bf = c12.bitcast(BF16)
        for h in range(HFFN):
            ncnow = s_c.n
            ntp = s_t.inc()
            def f(t, h=h, ncnow=ncnow):
                t.wait_ge(s_c.h, ncnow)
                t.matmul(PS[0][0:1, 0:NS], bq_bf[:, 2 * h:2 * h + 1], k1t,
                         start=True, stop=True)
                t.matmul(PS[0][0:1, NS:2 * NS], bq_bf[:, 2 * h + 1:2 * h + 2], k2t,
                         start=True, stop=True).then_inc(s_t.h, 1)
            te(f)
            ncev = s_c.inc()
            def f(v, ntp=ntp):
                v.wait_ge(s_t.h, ntp)
                v.tensor_copy(c12bf[0:1, 0:2 * NS], PS[0][0:1, 0:2 * NS])
                v.drain().then_inc(s_c.h, 1)
            ve(f)
            ntp2 = s_t.inc()
            def f(t, ncev=ncev):
                t.wait_ge(s_c.h, ncev)
                t.matmul(PS[1][:, 0:2 * NS], ones_bf[0:1, 0:128],
                         c12bf[0:1, 0:2 * NS], start=True, stop=True) \
                 .then_inc(s_t.h, 1)
            te(f)
            ncst = s_c.inc()
            def f(v, h=h, ntp2=ntp2):
                v.wait_ge(s_t.h, ntp2)
                v.tensor_copy(rep_c1[:, h * 2 * NS:(h + 1) * 2 * NS],
                              PS[1][:, 0:2 * NS])
                v.drain().then_inc(s_c.h, 1)
            ve(f)
        BK["rep_c1_c"] = s_c.n
        BK["p0_c"] = s_c.n
        BK["p0_t"] = s_t.n

    def p1():
        xn2_done = {}
        prev_evac = 0
        p0c, p0t = BK["p0_c"], BK["p0_t"]
        for t in range(NT_E):
            buf = t % 2
            xt = xload[:, buf * D:(buf + 1) * D]
            x2t = xn_tok[:, buf * D:(buf + 1) * D]
            need = xn2_done.get(t - 2, 0)
            sx = s_xp[buf]
            ndma = sx.inc(16)
            def f(e, xt=xt, t=t, need=need, sx=sx):
                if need:
                    e.wait_ge(s_s.h, need)
                e.dma_start(xt, xe[t * 128:(t + 1) * 128, :]).then_inc(sx.h, 16)
            sy(f)

            nst = s_v.inc()
            def f(v, xt=xt, ndma=ndma, t=t, p0t=p0t, sx=sx):
                v.wait_ge(sx.h, ndma)
                if t == 0:
                    v.wait_ge(s_t.h, p0t)
                v.tensor_scalar(out=dmpA, in0=xt, scalar1=1.0 / D, scalar2=0.0,
                                op0=OP.mult, op1=OP.add, accum_out=stat[:, 0:1])
                v.scalar_tensor_tensor(out=dmpA, in0=xt, scalar=1.0 / D, in1=xt,
                                       op0=OP.mult, op1=OP.mult, accum_out=stat[:, 1:2])
                v.drain()
                v.tensor_tensor(out=stat[:, 2:3], in0=stat[:, 0:1], in1=stat[:, 0:1],
                                op=OP.mult)
                v.drain()
                v.tensor_sub(stat[:, 3:4], stat[:, 1:2], stat[:, 2:3])
                v.drain()
                v.tensor_scalar(out=stat[:, 4:5], in0=stat[:, 3:4], scalar1=LN_EPS,
                                scalar2=None, op0=OP.add)
                v.tensor_scalar(out=stat[:, 5:6], in0=stat[:, 3:4],
                                scalar1=1.0 + RMS_EPS, scalar2=LN_EPS * RMS_EPS,
                                op0=OP.mult, op1=OP.add)
                v.drain().then_inc(s_v.h, 1)
            ve(f)

            nsq = s_s.inc()
            def f(s, nst=nst):
                s.wait_ge(s_v.h, nst)
                s.activation(out=stat[:, 6:8], in_=stat[:, 4:6], func=AF.Sqrt)
                s.drain().then_inc(s_s.h, 1)
            sa(f)

            nsb = s_v.inc()
            def f(v, nsq=nsq, t=t):
                v.wait_ge(s_s.h, nsq)
                v.reciprocal(stat[:, 8:10], stat[:, 6:8])
                v.drain()
                v.tensor_tensor(out=rinv[:, t:t + 1], in0=stat[:, 8:9], in1=stat[:, 7:8],
                                op=OP.mult)
                v.drain()
                v.tensor_scalar(out=stat[:, 10:11], in0=stat[:, 0:1],
                                scalar1=stat[:, 9:10], scalar2=-1.0,
                                op0=OP.mult, op1=OP.mult)
                v.drain().then_inc(s_v.h, 1)
            ve(f)

            nxn = s_s.inc()
            xn2_done[t] = nxn
            def f(s, xt=xt, x2t=x2t, nsb=nsb):
                s.wait_ge(s_v.h, nsb)
                s.activation(out=x2t, in_=xt, func=AF.Identity, bias=stat[:, 10:11],
                             scale=stat[:, 9:10])
                s.drain().then_inc(s_s.h, 1)
            sa(f)

            ntp = s_t.inc()
            def f(t_, x2t=x2t, nxn=nxn, need_ps=prev_evac, t=t, p0c=p0c):
                t_.wait_ge(s_s.h, nxn)
                if t == 0:
                    t_.wait_ge(s_c.h, p0c)
                if need_ps:
                    t_.wait_ge(s_v.h, need_ps)
                mm = None
                for ch in range(NCH):
                    mm = t_.transpose(PSB[0][:, ch * 128:(ch + 1) * 128],
                                      x2t[:, ch * 128:(ch + 1) * 128], ident)
                mm.then_inc(s_t.h, 1)
            te(f)

            nev = s_v.inc()
            def f(v, t=t, ntp=ntp):
                v.wait_ge(s_t.h, ntp)
                for ch in range(NCH):
                    v.tensor_copy(xn2fem[:, ch * E + t * 128:ch * E + (t + 1) * 128],
                                  PSB[0][:, ch * 128:(ch + 1) * 128])
                v.engine_nop()
                v.drain().then_inc(s_v.h, 1)
            ve(f)
            prev_evac = nev
        BK["p1_v"] = s_v.n
        BK["p1_s"] = s_s.n
        dotap("xn2fem", xn2fem, s_v, s_v.n)
        dotap("rinv", rinv[:, 0:NT_E], s_v, s_v.n)

    def p2():
        p1v = BK["p1_v"]
        gate_ev = []
        for qb in range(R // 512):
            bank = qb % 2
            ntm = s_t.inc()
            guard = gate_ev[-2] if len(gate_ev) >= 2 else 0
            def f(t, qb=qb, bank=bank, p1v=p1v, guard=guard):
                if qb == 0:
                    t.wait_ge(s_v.h, p1v)
                    t.wait_ge(s_c.h, BK["p0_c"])
                if guard:
                    t.wait_ge(s_s.h, guard)
                mm = None
                for ch in range(NCH):
                    mm = t.matmul(PS[bank][0:HATT, 0:512],
                                  wg_sb[:, ch * HATT:(ch + 1) * HATT],
                                  xn2fem[:, ch * E + qb * 512:ch * E + (qb + 1) * 512],
                                  start=(ch == 0), stop=(ch == NCH - 1))
                mm.then_inc(s_t.h, 1)
            te(f)
            nsg = s_s.inc()
            gate_ev.append(nsg)
            def f(s, qb=qb, bank=bank, ntm=ntm):
                s.wait_ge(s_t.h, ntm)
                s.activation(out=gstg[0:HATT, qb * 512:(qb + 1) * 512],
                             in_=PS[bank][0:HATT, 0:512], func=AF.Sigmoid)
                s.drain().then_inc(s_s.h, 1)
            sa(f)
        BK["gates_s"] = s_s.n
        dotap("gates", gstg[:, 0:R], s_s, s_s.n)

        stream_block.piece = 0
        stream_block.prev = [BK["p0_c"], BK["p0_c"]]
        gsn = BK["gates_s"]
        for h in range(HFFN):
            wfr = stream_block(wf_in, wfp, 0, NCH, h * 256, 256,
                               extra_s=BK.get("p2_ffq_last", 0))
            for gi in range(NQS):
                ntm = s_t.inc()
                bank_guard = BK.get("p2_ffq_last", 0)
                def f(t, h=h, gi=gi, wfr=wfr, gsn=gsn, bank_guard=bank_guard):
                    if gi == 0:
                        t.wait_ge(s_c.h, wfr)
                    t.wait_ge(s_s.h, max(gsn, bank_guard))
                    mm = None
                    for half in range(2):
                        for ch in range(NCH):
                            mm = t.matmul(PS[half][:, 0:512],
                                     wfp[:, ch * 256 + half * 128:ch * 256 + half * 128 + 128],
                                     xn2fem[:, ch * E + gi * 512:ch * E + (gi + 1) * 512],
                                     start=(ch == 0), stop=(ch == NCH - 1))
                    mm.then_inc(s_t.h, 1)
                te(f)
                nev = s_s.inc(2)
                def f(s, ntm=ntm):
                    s.wait_ge(s_t.h, ntm)
                    s.copy(ffnq[:, 0:512], PS[0][:, 0:512])
                    s.drain().then_inc(s_s.h, 1)
                    s.copy(ffnq[:, 512:1024], PS[1][:, 0:512])
                    s.drain().then_inc(s_s.h, 1)
                sa(f)
                BK["p2_ffq_last"] = nev
                if gi == 0 and h == 0:
                    dotap("ffnq", ffnq, s_s, s_s.n)
                    BK["p2_ffq_last"] = max(BK["p2_ffq_last"], 0)

                for tk_ in range(4):
                    t_glob = gi * 4 + tk_
                    p2_tile_head(tk_, t_glob, h)
        BK["p2_v"] = s_v.n
        dotap("ei", ei_all, s_v, s_v.n)
        dotap("sc", sc_all, s_v, s_v.n)

    def p2_tile_head(tk_, t_glob, h):
        ffq_evac = BK["p2_ffq_last"]
        fix_prev = BK.get("p2_fix_prev", 0)
        ntm = s_t.inc()
        def f(t, h=h, tk_=tk_, ffq_evac=ffq_evac, fix_prev=fix_prev):
            t.wait_ge(s_s.h, ffq_evac)
            if fix_prev:
                t.wait_ge(s_v.h, fix_prev)
            t.matmul(PS[2][:, 0:NS],
                     ffnq[:, tk_ * 128:(tk_ + 1) * 128],
                     k1t, start=True, stop=True)
            t.matmul(PS[2][:, NS:2 * NS],
                     ffnq[:, 512 + tk_ * 128:512 + (tk_ + 1) * 128],
                     k2t, start=True, stop=True).then_inc(s_t.h, 1)
        te(f)

        nfix = s_v.inc()
        BK["p2_fix_prev"] = nfix
        def f(v, t_glob=t_glob, h=h, ntm=ntm):
            v.wait_ge(s_t.h, ntm)
            v.wait_ge(s_c.h, BK["rep_c1_c"])
            v.scalar_tensor_tensor(out=s12, in0=PS[2][:, 0:2 * NS],
                                   scalar=rinv[:, t_glob:t_glob + 1],
                                   in1=rep_c1[:, h * 2 * NS:(h + 1) * 2 * NS],
                                   op0=OP.mult, op1=OP.add)
            v.drain().then_inc(s_v.h, 1)
        ve(f)
        if t_glob == 0 and h == 0:
            dotap("s12", s12, s_v, s_v.n)

        ntop = s_v.inc()
        def f(v, t_glob=t_glob, h=h):
            for half in range(2):
                src = s12[:, NS * half:NS * (half + 1)]
                rep = srep[:, NS * half:NS * (half + 1)]
                vv = topv[:, 16 * half:16 * half + 16]
                ii = topi[:, 16 * half:16 * half + 16]
                v.max(out=vv[:, 0:8], in_=src)
                v.drain()
                v.max_index(out=ii[:, 0:8], in_max=vv[:, 0:8], in_values=src)
                v.drain()
                v.match_replace(out=rep, in_to_replace=vv[:, 0:8], in_values=src,
                                imm_value=NEG)
                v.drain()
                v.max(out=vv[:, 8:16], in_=rep)
                v.drain()
                v.max_index(out=ii[:, 8:16], in_max=vv[:, 8:16], in_values=rep)
                v.drain()
            v.memset(comb, NEG)
            v.tensor_tensor(
                out=comb.rearrange("p (j l) -> p j l", l=16)[:, 0:10, 0:10],
                in0=topv[:, 0:10].unsqueeze(-1).broadcast_to([128, 10, 10]),
                in1=topv[:, 16:26].unsqueeze(1).broadcast_to([128, 10, 10]),
                op=OP.add)
            v.drain()
            v.max(out=cv8[:, 0:8], in_=comb)
            v.drain()
            v.match_replace(out=crep, in_to_replace=cv8[:, 0:8], in_values=comb,
                            imm_value=NEG)
            v.drain()
            v.max(out=cv8[:, 8:16], in_=crep)
            v.drain()
            v.max_index(out=ci8[:, 0:8], in_max=cv8[:, 0:8], in_values=comb)
            v.drain()
            v.max_index(out=ci8[:, 8:16], in_max=cv8[:, 8:16], in_values=crep)
            v.drain()
            v.tensor_copy(valst[:, 0:10], cv8[:, 0:10])
            v.tensor_copy(bji[:, 0:8], ci8[:, 0:8])
            v.tensor_copy(bji[:, 8:10], ci8[:, 8:10])
            v.drain()
            v.tensor_scalar(out=ci8[:, 0:10], in0=bji[:, 0:10], scalar1=4,
                            scalar2=None, op0=OP.logical_shift_right)
            v.drain()
            v.tensor_copy(bjf[:, 0:10], ci8[:, 0:10])
            v.drain()
            v.tensor_scalar(out=ci8[:, 0:10], in0=bji[:, 0:10], scalar1=15,
                            scalar2=None, op0=OP.bitwise_and)
            v.drain()
            v.tensor_copy(bjf[:, 10:20], ci8[:, 0:10])
            v.tensor_copy(i1f[:, 0:10], topi[:, 0:10])
            v.tensor_copy(i1f[:, 16:26], topi[:, 16:26])
            v.drain()
            for (bofs, iofs) in ((0, 0), (10, 16)):
                mm = mskt[:, 0:100].rearrange("p (k j) -> p k j", j=10)
                v.tensor_tensor(
                    out=mm,
                    in0=bjf[:, bofs:bofs + 10].unsqueeze(-1).broadcast_to([128, 10, 10]),
                    in1=iota10.rearrange("p (k j) -> p k j", j=10), op=OP.is_equal)
                v.drain()
                v.tensor_tensor(
                    out=mm, in0=mm,
                    in1=i1f[:, iofs:iofs + 10].unsqueeze(1).broadcast_to([128, 10, 10]),
                    op=OP.mult)
                v.drain()
                v.tensor_reduce(out=e12[:, bofs:bofs + 10], in_=mm, axis=AX.X, op=OP.add)
                v.drain()
            v.tensor_scalar(out=eff[:, 0:10], in0=e12[:, 0:10], scalar1=float(NS),
                            scalar2=None, op0=OP.mult)
            v.drain()
            v.tensor_tensor(out=eff[:, 0:10], in0=eff[:, 0:10], in1=e12[:, 10:20],
                            op=OP.add)
            v.drain()
            v.tensor_copy(ei_all[:, (t_glob * HFFN + h) * 10:(t_glob * HFFN + h) * 10 + 10],
                          eff[:, 0:10])
            v.reduce_max(stat[:, 14:15], valst[:, 0:10], axis=AX.X)
            v.drain()
            v.tensor_scalar(out=stat[:, 15:16], in0=stat[:, 14:15], scalar1=-1.0,
                            scalar2=None, op0=OP.mult)
            v.drain().then_inc(s_v.h, 1)
        ve(f)

        nse = s_s.inc()
        def f(s, ntop=ntop):
            s.wait_ge(s_v.h, ntop)
            s.activation(out=pval[:, 0:10], in_=valst[:, 0:10], func=AF.Exp,
                         bias=stat[:, 15:16], accum_out=stat[:, 13:14])
            s.drain().then_inc(s_s.h, 1)
        sa(f)
        nsc = s_v.inc()
        def f(v, t_glob=t_glob, h=h, nse=nse):
            v.wait_ge(s_s.h, nse)
            v.reciprocal(stat[:, 12:13], stat[:, 13:14])
            v.drain()
            v.tensor_scalar(out=sc_all[:, (t_glob * HFFN + h) * 10:(t_glob * HFFN + h) * 10 + 10],
                            in0=pval[:, 0:10], scalar1=stat[:, 12:13], scalar2=None,
                            op0=OP.mult)
            v.drain().then_inc(s_v.h, 1)
        ve(f)

    def p3():
        unit = 0
        um_done = {}
        p2v = BK["p2_v"]
        for t in range(NT_R):
            buf = t % 2
            xt = xn_tok[:, buf * D:(buf + 1) * D]
            prev_ev = BK.get("p3_xnev", 0)
            ntp = s_t.inc()
            def f(t_, t=t, prev_ev=prev_ev, p2v=p2v):
                if t == 0:
                    t_.wait_ge(s_v.h, p2v)
                if prev_ev:
                    t_.wait_ge(s_v.h, prev_ev)
                mm = None
                for ch in range(NCH):
                    mm = t_.transpose(PSB[1][:, ch * 128:(ch + 1) * 128],
                                      xn2fem[:, ch * E + t * 128:ch * E + (t + 1) * 128],
                                      ident)
                mm.then_inc(s_t.h, 1)
            te(f)
            nxe = s_v.inc()
            BK["p3_xnev"] = nxe
            def f(v, xt=xt, t=t, ntp=ntp):
                v.wait_ge(s_t.h, ntp)
                for ch in range(NCH):
                    v.tensor_scalar(out=xt[:, ch * 128:(ch + 1) * 128],
                                    in0=PSB[1][:, ch * 128:(ch + 1) * 128],
                                    scalar1=rinv[:, t:t + 1], scalar2=None, op0=OP.mult)
                v.engine_nop()
                v.drain().then_inc(s_v.h, 1)
            ve(f)
            if t == 0 and "hcol" in tap_d:
                pass

            silu_prev = BK.get("p3_silu", 0)
            for h in range(HFFN):
                for k in range(10):
                    u = unit
                    sl = u % NGP
                    gd_t = gbuf[:, (2 * sl) * D:(2 * sl + 1) * D]
                    gu_t = gbuf[:, (2 * sl + 1) * D:(2 * sl + 2) * D]
                    need_gc = max(0, u - NGP + 1)
                    need_um = um_done.get(u - NGP, 0)
                    ngdn = s_gdn[sl].inc(16)
                    ngup = s_gup[sl].inc(16)
                    idx_col = ei_all[:, (t * HFFN + h) * 10 + k:(t * HFFN + h) * 10 + k + 1]
                    def f(g, gd_t=gd_t, gu_t=gu_t, idx_col=idx_col, need_gc=need_gc,
                          need_um=need_um, p2v=p2v, u=u, sl=sl):
                        if u == 0:
                            g.wait_ge(s_v.h, p2v)
                        if need_gc:
                            g.wait_ge(s_gc.h, need_gc)
                        if need_um:
                            g.wait_ge(s_um.h, need_um)
                        g.indirect_dma_start(out=gd_t, out_offset=None, in_=wdn_in,
                                             in_offset=IndirectOffsetOnAxis(ap=idx_col, axis=0)) \
                         .then_inc(s_gdn[sl].h, 16)
                        g.indirect_dma_start(out=gu_t, out_offset=None, in_=wup_in,
                                             in_offset=IndirectOffsetOnAxis(ap=idx_col, axis=0)) \
                         .then_inc(s_gup[sl].h, 16)
                    gp(f)

                    kk = u % 16
                    nhd = s_v.inc()
                    s_gc.inc()
                    def f(v, u=u, kk=kk, xt=xt, gd_t=gd_t, sl=sl, ngdn=ngdn):
                        v.wait_ge(s_gdn[sl].h, ngdn)
                        v.scalar_tensor_tensor(out=dmp3, in0=gd_t, scalar=1.0, in1=xt,
                                               op0=OP.mult, op1=OP.mult,
                                               accum_out=hcol[:, kk:kk + 1])
                        v.drain()
                        v.engine_nop()
                        v.drain().then_inc(s_gc.h, 1)
                        v.engine_nop()
                        v.drain().then_inc(s_v.h, 1)
                    ve(f)

                    nge = s_s.inc()
                    def f(s, kk=kk, nhd=nhd):
                        s.wait_ge(s_v.h, nhd)
                        s.activation(out=hgel[:, kk:kk + 1], in_=hcol[:, kk:kk + 1],
                                     func=AF.Gelu)
                        s.drain().then_inc(s_s.h, 1)
                    sa(f)

                    need_um2 = um_done.get(u - 2, 0)
                    ndg = s_dg.inc()
                    def f(v, t=t, h=h, k=k, u=u, kk=kk, nge=nge, need_um2=need_um2):
                        v.wait_ge(s_s.h, nge)
                        if need_um2:
                            v.wait_ge(s_um.h, need_um2)
                        v.tensor_scalar(out=diag[:, (u % 2) * 128:(u % 2) * 128 + 128],
                                        in0=ident, scalar1=hgel[:, kk:kk + 1],
                                        scalar2=sc_all[:, (t * HFFN + h) * 10 + k:
                                                       (t * HFFN + h) * 10 + k + 1],
                                        op0=OP.mult, op1=OP.mult)
                        v.drain().then_inc(s_dg.h, 1)
                    ve(f)

                    hk = h * 10 + k
                    num = s_um.inc()
                    um_done[u] = num
                    def f(t_, u=u, hk=hk, ndg=ndg, silu_prev=silu_prev,
                          sl=u % NGP, ngup=ngup):
                        t_.wait_ge(s_dg.h, ndg)
                        t_.wait_ge(s_gup[sl].h, ngup)
                        if hk == 0 and silu_prev:
                            t_.wait_ge(s_s.h, silu_prev)
                        sl = u % NGP
                        gu_t = gbuf[:, (2 * sl + 1) * D:(2 * sl + 2) * D]
                        dg_t = diag[:, (u % 2) * 128:(u % 2) * 128 + 128]
                        mm = None
                        for (bi, c0, cw) in DSPL:
                            mm = t_.matmul(PS[3 + bi][:, 0:cw], dg_t, gu_t[:, c0:c0 + cw],
                                      start=(hk == 0), stop=(hk == HFFN * 10 - 1))
                        mm.then_inc(s_um.h, 1)
                    te(f)
                    unit += 1

            nsl = s_s.inc(len(DSPL))
            BK["p3_silu"] = nsl
            ulast = s_um.n
            def f(s, t=t, ulast=ulast):
                s.wait_ge(s_um.h, ulast)
                for (bi, c0, cw) in DSPL:
                    s.activation(out=g_out[:, t * D + c0:t * D + c0 + cw],
                                 in_=PS[3 + bi][:, 0:cw], func=AF.Silu)
                s.drain().then_inc(s_s.h, len(DSPL))
            sa(f)
        BK["p3_s"] = s_s.n
        dotap("gff", g_out, s_s, s_s.n)

    def stream_block(w_in, dst, rows0, nch, c0, cw, extra_s=0, extra_v=0,
                     fold=False, scale=1.0):
        for ch in range(nch):
            buf = stream_block.piece % 2
            st_ap = wld[:, buf * 512:buf * 512 + cw]
            need = stream_block.prev[buf]
            sw = s_wp[buf]
            ndma = sw.inc(16)
            def f(e, st_ap=st_ap, ch=ch, need=need, w_in=w_in, rows0=rows0, c0=c0,
                  cw=cw, extra_s=extra_s, extra_v=extra_v, sw=sw):
                e.wait_ge(s_c.h, need)
                if extra_s and ch == 0:
                    e.wait_ge(s_s.h, extra_s)
                if extra_v and ch == 0:
                    e.wait_ge(s_v.h, extra_v)
                e.dma_start(st_ap, w_in[rows0 + ch * 128:rows0 + ch * 128 + 128,
                                        c0:c0 + cw]).then_inc(sw.h, 16)
            sy(f)
            ncst = s_c.inc()
            def f(v, st_ap=st_ap, ch=ch, ndma=ndma, dst=dst, cw=cw, fold=fold,
                  scale=scale, sw=sw):
                v.wait_ge(sw.h, ndma)
                if fold:
                    v.tensor_scalar(out=dst[:, ch * cw:(ch + 1) * cw], in0=st_ap,
                                    scalar1=nw_f[:, ch:ch + 1], scalar2=float(scale),
                                    op0=OP.mult, op1=OP.mult)
                    v.drain().then_inc(s_c.h, 1)
                else:
                    v.tensor_copy(dst[:, ch * cw:(ch + 1) * cw], st_ap)
                    v.drain().then_inc(s_c.h, 1)
            ve(f)
            stream_block.prev[buf] = ncst
            stream_block.piece += 1
        return s_c.n

    def p4():
        stream_block.piece = 0
        stream_block.prev = [BK["p0_c"], BK["p0_c"]]
        p1v = BK["p1_v"]
        p3s = BK["p3_s"]
        BK["p4_glstg_hist"] = []
        BK["p4_unit"] = 0

        for hp in range(NPAIR):
            h0 = 2 * hp
            wkr = stream_block(wkv_in, wkvq_a, 0, NCH, h0 * DH, 128,
                               extra_s=(p3s if hp == 0 else 0),
                               extra_v=BK.get("p4_vev", 0), fold=True)
            prev_kev = BK.get("p4_kev", 0)
            ssn = s_s.n
            kev_hist = []
            for nb in range(E // 512):
                bank = 5 + nb % 2
                ntm = s_t.inc()
                guard = kev_hist[-2] if len(kev_hist) >= 2 else 0
                def f(t, nb=nb, bank=bank, wkr=wkr, prev_kev=prev_kev, p1v=p1v,
                      ssn=ssn, guard=guard):
                    if nb == 0:
                        t.wait_ge(s_c.h, wkr)
                        t.wait_ge(s_v.h, max(p1v, prev_kev))
                        t.wait_ge(s_s.h, ssn)
                    if guard:
                        t.wait_ge(s_v.h, guard)
                    mm = None
                    for ch in range(NCH):
                        mm = t.matmul(PS[bank][:, 0:512], wkvq_a[:, ch * 128:(ch + 1) * 128],
                                 xn2fem[:, ch * E + nb * 512:ch * E + (nb + 1) * 512],
                                 start=(ch == 0), stop=(ch == NCH - 1))
                    mm.then_inc(s_t.h, 1)
                te(f)
                nev = s_v.inc()
                kev_hist.append(nev)
                def f(v, nb=nb, bank=bank, ntm=ntm):
                    v.wait_ge(s_t.h, ntm)
                    v.tensor_copy(kpair[:, nb * 512:(nb + 1) * 512], PS[bank][:, 0:512])
                    v.drain().then_inc(s_v.h, 1)
                ve(f)
            if hp == 0:
                dotap("kpair", kpair, s_v, s_v.n)
            BK["p4_kev"] = s_v.n

            wqr = stream_block(wq_in, wkvq_a, 0, NCH, h0 * DH, 128,
                               extra_v=BK["p4_kev"], fold=True, scale=0.125)
            prev_qev = BK.get("p4_qev", 0)
            ssn = s_s.n
            qev_hist = []
            for nb in range(R // 512):
                bank = 5 + nb % 2
                ntm = s_t.inc()
                guard = qev_hist[-2] if len(qev_hist) >= 2 else 0
                def f(t, nb=nb, bank=bank, wqr=wqr, prev_qev=prev_qev, ssn=ssn,
                      guard=guard):
                    if nb == 0:
                        t.wait_ge(s_c.h, wqr)
                        if prev_qev:
                            t.wait_ge(s_v.h, prev_qev)
                        t.wait_ge(s_s.h, ssn)
                    if guard:
                        t.wait_ge(s_v.h, guard)
                    mm = None
                    for ch in range(NCH):
                        mm = t.matmul(PS[bank][:, 0:512], wkvq_a[:, ch * 128:(ch + 1) * 128],
                                 xn2fem[:, ch * E + nb * 512:ch * E + (nb + 1) * 512],
                                 start=(ch == 0), stop=(ch == NCH - 1))
                    mm.then_inc(s_t.h, 1)
                te(f)
                nev = s_v.inc()
                qev_hist.append(nev)
                def f(v, nb=nb, bank=bank, ntm=ntm):
                    v.wait_ge(s_t.h, ntm)
                    v.tensor_copy(qpair[:, nb * 512:(nb + 1) * 512], PS[bank][:, 0:512])
                    v.drain().then_inc(s_v.h, 1)
                ve(f)
            BK["p4_qev"] = s_v.n

            wvr = stream_block(wkv_in, wkvq_a, 0, NCH, DI + h0 * DH, 128,
                               extra_v=BK["p4_qev"], fold=True)
            prev_vev = BK.get("p4_vev", 0)
            ssn = s_s.n
            vev_hist = []
            for tt in range(NT_E):
                bank = 5 + tt % 2
                ntm = s_t.inc()
                guard = vev_hist[-2] if len(vev_hist) >= 2 else 0
                def f(t, tt=tt, bank=bank, wvr=wvr, prev_vev=prev_vev, ssn=ssn,
                      guard=guard):
                    if tt == 0:
                        t.wait_ge(s_c.h, wvr)
                        if prev_vev:
                            t.wait_ge(s_v.h, prev_vev)
                        t.wait_ge(s_s.h, ssn)
                    if guard:
                        t.wait_ge(s_v.h, guard)
                    mm = None
                    for ch in range(NCH):
                        mm = t.matmul(PS[bank][:, 0:128],
                                 xn2fem[:, ch * E + tt * 128:ch * E + (tt + 1) * 128],
                                 wkvq_a[:, ch * 128:(ch + 1) * 128],
                                 start=(ch == 0), stop=(ch == NCH - 1))
                    mm.then_inc(s_t.h, 1)
                te(f)
                nev = s_v.inc()
                vev_hist.append(nev)
                def f(v, tt=tt, bank=bank, ntm=ntm):
                    v.wait_ge(s_t.h, ntm)
                    for hh in range(2):
                        v.tensor_copy(v2[:, (hh * NT_E + tt) * 64:(hh * NT_E + tt) * 64 + 64],
                                      PS[bank][:, hh * 64:(hh + 1) * 64])
                    v.engine_nop()
                    v.drain().then_inc(s_v.h, 1)
                ve(f)
            BK["p4_vev"] = s_v.n

            for hh in range(2):
                h = 2 * hp + hh
                pe = 64 * hh
                for qs in range(NQS):
                    p4_head_qs(hp, hh, h, pe, qs)
        BK["p4_fz"] = s_fz.n
        dotap("gatedT", gatedT, s_fz, s_fz.n)

    def p4_head_qs(hp, hh, h, pe, qs):
        gqi = len(BK["p4_glstg_hist"])
        buf = gqi % 2
        sgp = s_sgp[buf]
        gmv = sgp.inc(16)
        gsr = max(BK["gates_s"], BK["p3_s"] if gqi == 0 else 0)
        guard = BK["p4_glstg_hist"][-2] if len(BK["p4_glstg_hist"]) >= 2 else 0
        def f(e, h=h, pe=pe, qs=qs, gsr=gsr, buf=buf, guard=guard, sgp=sgp):
            e.wait_ge(s_s.h, gsr)
            if guard:
                e.wait_ge(s_fz.h, guard)
            e.dma_start(glstg[pe:pe + 1, buf * 512:(buf + 1) * 512],
                        gstg[h:h + 1, qs * 512:(qs + 1) * 512]).then_inc(sgp.h, 16)
        sy(f)

        units = []
        for j in range(4 * qs):
            units.append((j, None))
        for a in range(4):
            units.append((4 * qs + a, ("A", a)))
        for j in range(4 * qs):
            units.append((NT_R + j, None))
        for a in range(4):
            units.append((NT_R + 4 * qs + a, ("B", a)))

        kevn = BK["p4_kev"]; qevn = BK["p4_qev"]; vevn = BK["p4_vev"]
        fz_prev = BK.get("p4_fz_prev", 0)
        exp_done = {}
        ptm_done = {}
        av_done = {}
        for ui, (kt, mk) in enumerate(units):
            sbank = 5 + (ui % 2)
            need_exp = exp_done.get(ui - 2, 0)
            ntm = s_t.inc()
            def f(t, kt=kt, sbank=sbank, need_exp=need_exp, kevn=kevn, qevn=qevn,
                  vevn=vevn, qs=qs, hh=hh, ui=ui, fz_prev=fz_prev):
                if ui == 0:
                    t.wait_ge(s_v.h, max(kevn, qevn, vevn))
                    if fz_prev:
                        t.wait_ge(s_fz.h, fz_prev)
                if need_exp:
                    t.wait_ge(s_s.h, need_exp)
                t.matmul(PS[sbank][:, 0:512],
                         kpair[64 * hh:64 * hh + 64, kt * 128:(kt + 1) * 128],
                         qpair[64 * hh:64 * hh + 64, qs * 512:(qs + 1) * 512],
                         start=True, stop=True,
                         tile_position=(64 * hh, 0) if hh else None) \
                 .then_inc(s_t.h, 1)
            te(f)

            slot = ui % NPT
            need_av = av_done.get(ui - NPT, 0)
            nse = s_s.inc()
            exp_done[ui] = nse
            def f(s, ntm_=ntm, sbank=sbank, slot=slot, need_av=need_av):
                s.wait_ge(s_t.h, ntm_)
                if need_av:
                    s.wait_ge(s_av.h, need_av)
                s.activation(out=pt[:, slot * 512:(slot + 1) * 512],
                             in_=PS[sbank][:, 0:512], func=AF.Exp, bias=cexp[:, 0:1])
                s.drain().then_inc(s_s.h, 1)
            sa(f)

            if mk is not None:
                npm = s_v.inc()
                ptm_done[ui] = ("v", npm)
                def f(v, nse=nse, slot=slot, kind=mk[0], a=mk[1]):
                    v.wait_ge(s_s.h, nse)
                    ps_ = pt[:, slot * 512:(slot + 1) * 512]
                    if a:
                        v.memset(ps_[:, 0:128 * a], 0.0)
                    blk = ps_[:, 128 * a:128 * (a + 1)]
                    if kind == "A":
                        v.tensor_tensor(out=blk, in0=blk, in1=tril, op=OP.mult)
                    else:
                        v.tensor_scalar(out=blk, in0=blk, scalar1=cfl[:, 0:1],
                                        scalar2=None, op0=OP.mult)
                    v.engine_nop()
                    v.drain().then_inc(s_v.h, 1)
                ve(f)
            else:
                ptm_done[ui] = ("s", nse)

            nav = s_av.inc()
            av_done[ui] = nav
            eng, val = ptm_done[ui]
            last = (ui == len(units) - 1)
            def f(t, kt=kt, slot=slot, hh=hh, pe=pe, ui=ui, eng=eng, val=val, last=last):
                if eng == "s":
                    t.wait_ge(s_s.h, val)
                else:
                    t.wait_ge(s_v.h, val)
                vt = v2[:, (hh * NT_E + kt) * 64:(hh * NT_E + kt) * 64 + 64]
                pts = pt[:, slot * 512:(slot + 1) * 512]
                if pe == 0:
                    t.matmul(PS[7][0:64, 0:512], vt, pts, start=(ui == 0), stop=last)
                    t.matmul(PS[2][0:1, 0:512], onescol, pts, start=(ui == 0),
                             stop=last).then_inc(s_av.h, 1)
                else:
                    t.matmul(PS[7][64:128, 0:512], vt, pts, start=(ui == 0), stop=last,
                             tile_position=(0, 64))
                    t.matmul(PS[2][64:65, 0:512], onescol, pts, start=(ui == 0),
                             stop=last, tile_position=(0, 64)).then_inc(s_av.h, 1)
            te(f)

        navl = s_av.n
        nfz1 = s_v.inc()
        def f(v, pe=pe, navl=navl, gmv=gmv, buf=buf, sgp=sgp):
            v.wait_ge(s_av.h, navl)
            v.wait_ge(sgp.h, gmv)
            v.reciprocal(lrow[pe:pe + 1, 0:512], PS[2][pe:pe + 1, 0:512])
            v.drain()
            v.tensor_tensor(out=glrow[pe:pe + 1, 0:512], in0=lrow[pe:pe + 1, 0:512],
                            in1=glstg[pe:pe + 1, buf * 512:(buf + 1) * 512],
                            op=OP.mult)
            v.drain().then_inc(s_v.h, 1)
        ve(f)
        ntr = s_t.inc()
        def f(t, pe=pe, nfz1=nfz1):
            t.wait_ge(s_v.h, nfz1)
            if pe == 0:
                t.matmul(PS[2][0:64, 0:512], ones_bf[0:1, 0:64], glrow[0:1, 0:512],
                         start=True, stop=True).then_inc(s_t.h, 1)
            else:
                t.matmul(PS[2][64:128, 0:512], ones_bf[64:65, 0:64],
                         glrow[64:65, 0:512], start=True, stop=True,
                         tile_position=(64, 64)).then_inc(s_t.h, 1)
        te(f)
        nfe = s_s.inc()
        def f(s, pe=pe, ntr=ntr):
            s.wait_ge(s_t.h, ntr)
            s.copy(fin[pe:pe + 64, 0:512], PS[2][pe:pe + 64, 0:512])
            s.drain().then_inc(s_s.h, 1)
        sa(f)
        nfz = s_fz.inc()
        BK["p4_fz_prev"] = nfz
        BK["p4_glstg_hist"].append(nfz)
        def f(v, pe=pe, hp=hp, qs=qs, nfe=nfe):
            v.wait_ge(s_s.h, nfe)
            v.tensor_tensor(out=gatedT[pe:pe + 64, hp * R + qs * 512:hp * R + (qs + 1) * 512],
                            in0=PS[7][pe:pe + 64, 0:512], in1=fin[pe:pe + 64, 0:512],
                            op=OP.mult)
            v.drain().then_inc(s_fz.h, 1)
        ve(f)

    def p5():
        stream_block.piece = 0
        stream_block.prev = [s_c.n, s_c.n]
        p4fz = BK["p4_fz"]
        first = True
        for ch in range(NDI):
            for (_, c0, cw) in DSPL:
                buf = stream_block.piece % 2
                st_ap = wld[:, buf * 512:buf * 512 + cw]
                need = stream_block.prev[buf]
                sw = s_wp[buf]
                ndma = sw.inc(16)
                def f(e, st_ap=st_ap, ch=ch, c0=c0, cw=cw, need=need, first=first,
                      p4fz=p4fz, sw=sw):
                    e.wait_ge(s_c.h, need)
                    if first:
                        e.wait_ge(s_fz.h, p4fz)
                    e.dma_start(st_ap, wout_in[ch * 128:ch * 128 + 128, c0:c0 + cw]) \
                     .then_inc(sw.h, 16)
                sy(f)
                ncst = s_c.inc()
                def f(v, st_ap=st_ap, ch=ch, c0=c0, cw=cw, ndma=ndma, sw=sw):
                    v.wait_ge(sw.h, ndma)
                    v.tensor_copy(wout_a[:, ch * D + c0:ch * D + c0 + cw], st_ap)
                    v.drain().then_inc(s_c.h, 1)
                ve(f)
                stream_block.prev[buf] = ncst
                stream_block.piece += 1
                first = False
        wor = s_c.n
        p4fzn = BK["p4_fz"]

        ep_done = {}
        for t in range(NT_R):
            buf = t % 2
            xt = xload[:, buf * D:(buf + 1) * D]
            need = ep_done.get(t - 2, 0)
            sx = s_xp[buf]
            ndma = sx.inc(16)
            def f(e, xt=xt, t=t, need=need, p4fzn=p4fzn, sx=sx):
                if t == 0:
                    e.wait_ge(s_fz.h, p4fzn)
                if need:
                    e.wait_ge(s_v.h, need)
                e.dma_start(xt, xe[t * 128:(t + 1) * 128, :]).then_inc(sx.h, 16)
            sy(f)

            ntm = s_t.inc()
            ep_prev = BK.get("p5_ep_prev", 0)
            def f(t_, t=t, wor=wor, p4fzn=p4fzn, ep_prev=ep_prev):
                if t == 0:
                    t_.wait_ge(s_c.h, wor)
                    t_.wait_ge(s_fz.h, p4fzn)
                if ep_prev:
                    t_.wait_ge(s_v.h, ep_prev)
                mm = None
                for pr in range(NPAIR):
                    for (bi, c0, cw) in DSPL:
                        mm = t_.matmul(PS[bi][:, 0:cw],
                                  gatedT[:, pr * R + t * 128:pr * R + (t + 1) * 128],
                                  wout_a[:, pr * D + c0:pr * D + c0 + cw],
                                  start=(pr == 0), stop=(pr == NPAIR - 1))
                mm.then_inc(s_t.h, 1)
            te(f)

            ot = otmp2.bitcast(F32)[:, buf * D:(buf + 1) * D]
            stp = s_stp[buf]
            st_prev = stp.n
            nep = s_v.inc()
            ep_done[t] = nep
            BK["p5_ep_prev"] = nep
            def f(v, t=t, xt=xt, ntm=ntm, ndma=ndma, ot=ot, stp=stp, st_prev=st_prev,
                  sx=sx):
                v.wait_ge(s_t.h, ntm)
                v.wait_ge(sx.h, ndma)
                if st_prev:
                    v.wait_ge(stp.h, st_prev)
                for (bi, c0, cw) in DSPL:
                    v.tensor_tensor(out=ot[:, c0:c0 + cw], in0=PS[bi][:, 0:cw],
                                    in1=g_out[:, t * D + c0:t * D + c0 + cw], op=OP.mult)
                v.tensor_tensor(out=ot, in0=ot, in1=xt, op=OP.add)
                v.drain().then_inc(s_v.h, 1)
            ve(f)

            nst = stp.inc(16)
            def f(e, t=t, nep=nep, ot=ot, stp=stp):
                e.wait_ge(s_v.h, nep)
                e.dma_start(out_d[t * 128:(t + 1) * 128, :], ot).then_inc(stp.h, 16)
            sy(f)

        nst0, nst1 = s_stp[0].n, s_stp[1].n
        nstf = s_st.n
        def f(e, nstf=nstf, nst0=nst0, nst1=nst1):
            if nstf:
                e.wait_ge(s_st.h, nstf)
            e.wait_ge(s_stp[0].h, nst0)
            e.wait_ge(s_stp[1].h, nst1)
        sy(f)

    nph = cfg.get("nphases", 6)
    with nc.Block() as b:
        blkref["b"] = b
        phases = [p0, p1, p2, p3, p4, p5]
        for ph in phases[:nph]:
            ph()

    nc.finalize()
    return nc


_CACHE = {}


def _perm_for(half, NT_R):
    own, other = [], []
    for i in range(NT_R):
        own.extend(range(256 * i + 128 * half, 256 * i + 128 * half + 128))
        other.extend(range(256 * i + 128 * (1 - half), 256 * i + 128 * (1 - half) + 128))
    return np.array(own + other, dtype=np.int64), np.array(own, dtype=np.int64)


def kernel(L0_attn_tokens, L0_attn_norm_w, L0_attn_Wq, L0_attn_Wkv, L0_attn_Wout,
           L0_attn_Wg, L0_ffn_Wq, L0_ffn_bq, L0_ffn_K1, L0_ffn_K2, L0_ffn_W_up,
           L0_ffn_W_down):
    cfg = full_cfg()
    NT_R = cfg["NT_R"]
    x = np.ascontiguousarray(np.asarray(L0_attn_tokens, dtype=np.float32))
    Bb, S, D = x.shape
    weights = dict(
        nw=np.ascontiguousarray(L0_attn_norm_w, dtype=np.float32),
        wq=np.ascontiguousarray(L0_attn_Wq, dtype=np.float32),
        wkv=np.ascontiguousarray(L0_attn_Wkv, dtype=np.float32),
        wout=np.ascontiguousarray(L0_attn_Wout, dtype=np.float32),
        wg=np.ascontiguousarray(L0_attn_Wg, dtype=np.float32),
        wf=np.ascontiguousarray(L0_ffn_Wq, dtype=np.float32),
        bq=np.ascontiguousarray(L0_ffn_bq, dtype=np.float32),
        k1=np.ascontiguousarray(L0_ffn_K1, dtype=np.float32),
        k2=np.ascontiguousarray(L0_ffn_K2, dtype=np.float32),
        wdn=np.ascontiguousarray(L0_ffn_W_down, dtype=np.float32),
        wup=np.ascontiguousarray(L0_ffn_W_up, dtype=np.float32),
    )
    if "nc" not in _CACHE:
        _CACHE["nc"] = build(cfg)
    nc = _CACHE["nc"]

    in_maps = []
    perms = []
    for c in range(8):
        b, half = c // 2, c % 2
        perm, own = _perm_for(half, NT_R)
        perms.append(own)
        m = dict(weights)
        m["xe"] = np.ascontiguousarray(x[b][perm])
        m["cflag"] = np.full((128, 1), float(half), dtype=np.float32)
        in_maps.append(m)

    res = run_bass_kernel_spmd(nc, in_maps, core_ids=list(range(8)))
    out = np.empty((Bb, S, D), dtype=np.float32)
    for c in range(8):
        b = c // 2
        out[b][perms[c]] = res.results[c]["out"]
    return out


# version 05
# speedup vs baseline: 1.3030x; 1.3030x over previous
import numpy as np
import concourse.bass as bass
import concourse.mybir as mybir
from concourse.bass import IndirectOffsetOnAxis
from concourse.bass_utils import run_bass_kernel_spmd

F32 = mybir.dt.float32
BF16 = mybir.dt.bfloat16
I32 = mybir.dt.int32
U32 = mybir.dt.uint32
U8 = mybir.dt.uint8
AF = mybir.ActivationFunctionType
OP = mybir.AluOpType
AX = mybir.AxisListType

NEG = -3.0e38
EXP_OFF = -4.0
LN_EPS = 1e-5
RMS_EPS = 1.1920929e-07


def full_cfg():
    return dict(D=1024, HATT=16, HFFN=8, NS=256, NEXP=65536, NT_R=16, taps=())


class Sem:
    def __init__(self, nc, name):
        self.h = nc.alloc_semaphore(name)
        self.n = 0

    def inc(self, k=1):
        self.n += k
        return self.n


def build(cfg):
    D = cfg["D"]; NCH = D // 128
    HATT = cfg["HATT"]; DH = 64
    HFFN = cfg["HFFN"]; NS = cfg["NS"]; NEXP = cfg["NEXP"]
    NT_R = cfg["NT_R"]; R = NT_R * 128
    NT_E = 2 * NT_R; E = 2 * R
    NQS = NT_R // 4
    DI = HATT * DH
    NDI = DI // 128
    KV = 2 * DI
    FQ = HFFN * 256
    NFC = FQ // 128
    NPAIR = HATT // 2
    NSH = NS // 128
    DSPL = [(i, i * 512, min(512, D - i * 512)) for i in range((D + 511) // 512)]
    taps = set(cfg.get("taps", ()))

    nc = bass.Bass("TRN2", target_bir_lowering=False, detect_race_conditions=False)

    xe = nc.dram_tensor("xe", [E, D], F32, kind="ExternalInput").ap()
    cflag = nc.dram_tensor("cflag", [128, 1], F32, kind="ExternalInput").ap()
    nw_in = nc.dram_tensor("nw", [D], F32, kind="ExternalInput").ap()
    wq_in = nc.dram_tensor("wq", [D, DI], F32, kind="ExternalInput").ap()
    wkv_in = nc.dram_tensor("wkv", [D, KV], F32, kind="ExternalInput").ap()
    wout_in = nc.dram_tensor("wout", [DI, D], F32, kind="ExternalInput").ap()
    wg_in = nc.dram_tensor("wg", [D, HATT], F32, kind="ExternalInput").ap()
    wf_in = nc.dram_tensor("wf", [D, FQ], F32, kind="ExternalInput").ap()
    bq_in = nc.dram_tensor("bq", [FQ], F32, kind="ExternalInput").ap()
    k1_in = nc.dram_tensor("k1", [NS, 128], F32, kind="ExternalInput").ap()
    k2_in = nc.dram_tensor("k2", [NS, 128], F32, kind="ExternalInput").ap()
    wdn_in = nc.dram_tensor("wdn", [NEXP, D], F32, kind="ExternalInput").ap()
    wup_in = nc.dram_tensor("wup", [NEXP, D], F32, kind="ExternalInput").ap()
    out_d = nc.dram_tensor("out", [R, D], F32, kind="ExternalOutput").ap()
    tap_d = {}
    def mktap(name, cols, dtype=F32):
        if name in taps:
            tap_d[name] = nc.dram_tensor("t_" + name, [128, cols], dtype,
                                         kind="ExternalOutput").ap()
    mktap("xn2fem", NCH * E, BF16); mktap("rinv", NT_E); mktap("ffnq", 2 * 512, BF16)
    mktap("s12", 2 * NS); mktap("ei", NT_R * HFFN * 10, I32)
    mktap("sc", NT_R * HFFN * 10); mktap("gff", NT_R * D, BF16)
    mktap("gates", R, BF16); mktap("gatedT", NPAIR * R, BF16)
    mktap("kpair", E, BF16)

    def dotap(name, src_ap, wait_sem, wait_val):
        if name not in tap_d:
            return
        tv = s_st.inc(16)
        def f(e, src_ap=src_ap, wait_sem=wait_sem, wait_val=wait_val, name=name):
            e.wait_ge(wait_sem.h, wait_val)
            e.dma_start(tap_d[name], src_ap).then_inc(s_st.h, 16)
        sy(f)
        s_v.inc()
        def f(v, tv=tv):
            v.wait_ge(s_st.h, tv)
            v.engine_nop()
            v.drain().then_inc(s_v.h, 1)
        ve(f)

    A = nc.alloc_sbuf_tensor
    xn2fem = A("xn2fem", [128, NCH * E], BF16).ap()
    g_out = A("g_out", [128, NT_R * D], BF16).ap()
    gatedT = A("gatedT", [128, NPAIR * R], BF16).ap()
    ei_all = A("ei_all", [128, NT_R * HFFN * 10], I32).ap()
    sc_all = A("sc_all", [128, NT_R * HFFN * 10], F32).ap()
    gstg = A("gstg", [128, R], BF16).ap()
    rep_c1 = A("rep_c1", [128, HFFN * 2 * NS], BF16).ap()
    wg_sb = A("wg_sb", [128, NCH * HATT], BF16).ap()
    k1t = A("k1t", [128, NS], BF16).ap()
    k2t = A("k2t", [128, NS], BF16).ap()
    ident = A("ident", [128, 128], BF16).ap()
    ones_bf = A("ones_bf", [128, 128], BF16).ap()
    onescol = A("onescol", [128, 1], BF16).ap()
    tril = A("tril", [128, 128], BF16).ap()
    iota10 = A("iota10", [128, 100], F32).ap()
    cfl = A("cfl", [128, 1], F32).ap()
    nw_f = A("nw_f", [128, NCH], F32).ap()
    bq_f = A("bq_f", [128, 2 * HFFN], F32).ap()
    bq_bf = A("bq_bf", [128, 2 * HFFN], BF16).ap()
    c12 = A("c12", [128, 2 * NS], F32).ap()
    rinv = A("rinv", [128, NT_E], F32).ap()
    stat = A("stat", [128, 16], F32).ap()
    cexp = A("cexp", [128, 1], F32).ap()
    wld = A("wld", [128, 2 * 512], F32).ap()
    topv = A("topv", [128, 32], F32).ap()
    topi = A("topi", [128, 32], U32).ap()
    comb = A("comb", [128, 160], F32).ap()
    crep = A("crep", [128, 160], F32).ap()
    cv8 = A("cv8", [128, 16], F32).ap()
    ci8 = A("ci8", [128, 16], U32).ap()
    bji = A("bji", [128, 16], U32).ap()
    bjf = A("bjf", [128, 32], F32).ap()
    i1f = A("i1f", [128, 32], F32).ap()
    mskt = A("mskt", [128, 200], F32).ap()
    e12 = A("e12", [128, 20], F32).ap()
    eff = A("eff", [128, 16], F32).ap()
    valst = A("valst", [128, 16], F32).ap()
    pval = A("pval", [128, 16], F32).ap()
    hcol = A("hcol", [128, 16], F32).ap()
    hgel = A("hgel", [128, 16], F32).ap()
    diag = A("diag", [128, 2 * 128], BF16).ap()
    dmpA = A("dmpA", [128, D], BF16).ap()

    AR = 40 * 1024
    arena = A("arena", [128, AR], U8).ap()

    def carve(off, n, dtype):
        sz = 4 if dtype in (F32, I32, U32) else 2
        assert off + n * sz <= AR, (off, n)
        return arena[:, off:off + n * sz].bitcast(dtype)

    wfp = carve(0, NCH * 256, BF16)
    ffnq = carve(NCH * 256 * 2, 2 * 512, BF16)
    _o2 = NCH * 256 * 2 + 2 * 512 * 2
    s12 = carve(_o2, 2 * NS, F32)
    srep = carve(_o2 + 2 * NS * 4, 2 * NS, F32)
    assert _o2 + 4 * NS * 4 <= 28 * 1024

    NGP = 5
    gbuf = carve(0, 2 * NGP * D, BF16)
    dmp3 = carve(2 * NGP * D * 2, D, BF16)
    assert 2 * NGP * D * 2 + D * 2 <= 28 * 1024
    xn_tok = carve(28 * 1024, 2 * D, BF16)

    _o = 0
    wkvq_a = carve(_o, NCH * 128, BF16); _o += NCH * 128 * 2
    kpair = carve(_o, E, BF16); _o += E * 2
    v2 = carve(_o, 2 * NT_E * 64, BF16); _o += 2 * NT_E * 64 * 2
    qpair = carve(_o, R, BF16); _o += R * 2
    NPT = 8
    pt = carve(_o, 3 * 512, BF16); _o += 3 * 512 * 2
    lrow = carve(_o, 512, F32); _o += 512 * 4
    glrow = carve(_o, 512, BF16)
    fin = carve(_o, 512, BF16); _o += 512 * 2
    assert _o <= 28 * 1024, _o
    glstg = carve(28 * 1024, 2 * 512, BF16)
    PTS = [pt[:, j * 512:(j + 1) * 512] for j in range(3)]
    PTS.append(carve(30 * 1024, 512, BF16))
    PTS += [carve(32 * 1024 + (j - 4) * 1024, 512, BF16) for j in range(4, 8)]

    wout_a = carve(0, NDI * D, BF16)
    otmp2 = carve(NDI * D * 2, 2 * D, F32)
    assert NDI * D * 2 + 2 * D * 4 <= 28 * 1024

    xload = carve(AR - 2 * D * 4, 2 * D, F32)
    assert 32 * 1024 <= AR - 2 * D * 4

    PS = [nc.alloc_psum_tensor(f"ps{i}", [128, 512], F32).ap() for i in range(8)]
    PSB = [p.bitcast(BF16) for p in PS]

    names = ["s_x", "s_w", "s_gp", "s_c", "s_v", "s_s", "s_t", "s_gd", "s_gc",
             "s_um", "s_dg", "s_pt", "s_av", "s_fz", "s_sg", "s_st"]
    SM = {n: Sem(nc, n) for n in names}
    (s_x, s_w, s_gp, s_c, s_v, s_s, s_t, s_gd, s_gc, s_um, s_dg, s_pt, s_av,
     s_fz, s_sg, s_st) = [SM[n] for n in names]
    s_xp = [Sem(nc, f"s_xp{i}") for i in range(2)]
    s_wp = [Sem(nc, f"s_wp{i}") for i in range(2)]
    s_sgp = [Sem(nc, f"s_sgp{i}") for i in range(2)]
    s_stp = [Sem(nc, f"s_stp{i}") for i in range(2)]
    s_gdn = [Sem(nc, f"s_gdn{i}") for i in range(NGP)]
    s_gup = [Sem(nc, f"s_gup{i}") for i in range(NGP)]

    BK = {}
    blkref = {}
    def sy(f): blkref["b"].sync(f)
    def gp(f): blkref["b"].gpsimd(f)
    def ve(f): blkref["b"].vector(f)
    def sa(f): blkref["b"].scalar(f)
    def te(f): blkref["b"].tensor(f)

    def _p0fin():
        BK["rep_c1_c"] = s_c.n
        BK["p0_c"] = s_c.n
        BK["p0_t"] = s_t.n

    def p0():
        def f(g):
            g.iota(iota10.rearrange("p (k j) -> p k j", j=10), [[0, 10], [1, 10]],
                   channel_multiplier=0, allow_small_or_imprecise_dtypes=True)
            g.iota(mskt[:, 0:128], [[1, 128]], channel_multiplier=-1,
                   allow_small_or_imprecise_dtypes=True).then_inc(s_gp.h, s_gp.inc())
        gp(f)

        nx = s_x.inc(48)
        def f(e):
            e.dma_start(cfl, cflag).then_inc(s_x.h, 16)
            e.dma_start(nw_f, nw_in.rearrange("(c p) -> p c", p=128),
                        allow_slow_non_contiguous=True).then_inc(s_x.h, 16)
            e.dma_start(bq_f, bq_in.rearrange("(hh d) -> d hh", d=128),
                        allow_slow_non_contiguous=True).then_inc(s_x.h, 16)
        sy(f)

        ngp, nxc = s_gp.n, nx
        ncn = s_c.inc()
        def f(v):
            v.wait_ge(s_gp.h, ngp)
            v.wait_ge(s_x.h, nxc)
            v.tensor_scalar(out=ident, in0=mskt[:, 0:128], scalar1=0.0, scalar2=None,
                            op0=OP.is_equal)
            v.tensor_scalar(out=tril, in0=mskt[:, 0:128], scalar1=0.0, scalar2=None,
                            op0=OP.is_ge)
            v.memset(ones_bf, 1.0)
            v.memset(onescol, 1.0)
            v.memset(cexp, EXP_OFF)
            v.memset(gstg, 0.0)
            v.memset(sc_all, 0.0)
            v.memset(hcol, 0.0)
            v.tensor_copy(bq_bf, bq_f)
            v.engine_nop()
            v.drain().then_inc(s_c.h, 1)
        ve(f)
        BK["consts"] = ncn
        if cfg.get('p0cut') == 1: return _p0fin()

        prev_cast = [ncn, ncn]
        piece = 0
        for ch in range(NCH):
            buf = piece % 2
            st_ap = wld[:, buf * 512:buf * 512 + HATT]
            need = prev_cast[buf]
            sw = s_wp[buf]
            ndma = sw.inc(16)
            def f(e, st_ap=st_ap, ch=ch, need=need, sw=sw):
                e.wait_ge(s_c.h, need)
                e.dma_start(st_ap, wg_in[ch * 128:(ch + 1) * 128, :]).then_inc(sw.h, 16)
            sy(f)
            ncst = s_c.inc()
            def f(v, st_ap=st_ap, ch=ch, ndma=ndma, sw=sw):
                v.wait_ge(sw.h, ndma)
                v.tensor_scalar(out=wg_sb[:, ch * HATT:(ch + 1) * HATT], in0=st_ap,
                                scalar1=nw_f[:, ch:ch + 1], scalar2=None, op0=OP.mult)
                v.drain().then_inc(s_c.h, 1)
            ve(f)
            prev_cast[buf] = ncst
            piece += 1
        if cfg.get('p0cut') == 2: return _p0fin()

        for (k_in, k_t) in ((k1_in, k1t), (k2_in, k2t)):
            for half in range(NSH):
                buf = piece % 2
                st_ap = wld[:, buf * 512:buf * 512 + 128]
                need = prev_cast[buf]
                sw = s_wp[buf]
                ndma = sw.inc(16)
                def f(e, st_ap=st_ap, k_in=k_in, half=half, need=need, sw=sw):
                    e.wait_ge(s_c.h, need)
                    e.dma_start(st_ap, k_in[half * 128:half * 128 + 128, :]) \
                     .then_inc(sw.h, 16)
                sy(f)
                ncst = s_c.inc()
                def f(v, st_ap=st_ap, half=half, ndma=ndma, sw=sw):
                    v.wait_ge(sw.h, ndma)
                    v.tensor_copy(dmpA[:, half * 128:half * 128 + 128], st_ap)
                    v.drain().then_inc(s_c.h, 1)
                ve(f)
                prev_cast[buf] = ncst
                piece += 1
            ncnow = s_c.n
            ntp = s_t.inc()
            def f(t, ncnow=ncnow):
                t.wait_ge(s_c.h, ncnow)
                mm = None
                for half in range(NSH):
                    mm = t.transpose(PSB[0][:, half * 128:half * 128 + 128],
                                     dmpA[:, half * 128:half * 128 + 128], ident)
                mm.then_inc(s_t.h, 1)
            te(f)
            ncst = s_c.inc()
            def f(v, k_t=k_t, ntp=ntp):
                v.wait_ge(s_t.h, ntp)
                v.tensor_copy(k_t, PSB[0][:, 0:NS])
                v.drain().then_inc(s_c.h, 1)
            ve(f)
            prev_cast[0] = prev_cast[1] = ncst
        if cfg.get('p0cut') == 3: return _p0fin()

        c12bf = c12.bitcast(BF16)
        for h in range(HFFN):
            ncnow = s_c.n
            ntp = s_t.inc()
            def f(t, h=h, ncnow=ncnow):
                t.wait_ge(s_c.h, ncnow)
                t.matmul(PS[0][0:1, 0:NS], bq_bf[:, 2 * h:2 * h + 1], k1t,
                         start=True, stop=True)
                t.matmul(PS[0][0:1, NS:2 * NS], bq_bf[:, 2 * h + 1:2 * h + 2], k2t,
                         start=True, stop=True).then_inc(s_t.h, 1)
            te(f)
            ncev = s_c.inc()
            def f(v, ntp=ntp):
                v.wait_ge(s_t.h, ntp)
                v.tensor_copy(c12bf[0:1, 0:2 * NS], PS[0][0:1, 0:2 * NS])
                v.drain().then_inc(s_c.h, 1)
            ve(f)
            ntp2 = s_t.inc()
            def f(t, ncev=ncev):
                t.wait_ge(s_c.h, ncev)
                t.matmul(PS[1][:, 0:2 * NS], ones_bf[0:1, 0:128],
                         c12bf[0:1, 0:2 * NS], start=True, stop=True) \
                 .then_inc(s_t.h, 1)
            te(f)
            ncst = s_c.inc()
            def f(v, h=h, ntp2=ntp2):
                v.wait_ge(s_t.h, ntp2)
                v.tensor_copy(rep_c1[:, h * 2 * NS:(h + 1) * 2 * NS],
                              PS[1][:, 0:2 * NS])
                v.drain().then_inc(s_c.h, 1)
            ve(f)
        BK["rep_c1_c"] = s_c.n
        BK["p0_c"] = s_c.n
        BK["p0_t"] = s_t.n

    def p1():
        xn2_done = {}
        prev_evac = 0
        p0c, p0t = BK["p0_c"], BK["p0_t"]
        for t in range(NT_E):
            buf = t % 2
            xt = xload[:, buf * D:(buf + 1) * D]
            x2t = xn_tok[:, buf * D:(buf + 1) * D]
            need = xn2_done.get(t - 2, 0)
            sx = s_xp[buf]
            ndma = sx.inc(16)
            def f(e, xt=xt, t=t, need=need, sx=sx):
                if need:
                    e.wait_ge(s_s.h, need)
                e.dma_start(xt, xe[t * 128:(t + 1) * 128, :]).then_inc(sx.h, 16)
            sy(f)

            nst = s_v.inc()
            def f(v, xt=xt, ndma=ndma, t=t, p0t=p0t, sx=sx):
                v.wait_ge(sx.h, ndma)
                if t == 0:
                    v.wait_ge(s_t.h, p0t)
                v.tensor_scalar(out=dmpA, in0=xt, scalar1=1.0 / D, scalar2=0.0,
                                op0=OP.mult, op1=OP.add, accum_out=stat[:, 0:1])
                v.scalar_tensor_tensor(out=dmpA, in0=xt, scalar=1.0 / D, in1=xt,
                                       op0=OP.mult, op1=OP.mult, accum_out=stat[:, 1:2])
                v.drain()
                v.tensor_tensor(out=stat[:, 2:3], in0=stat[:, 0:1], in1=stat[:, 0:1],
                                op=OP.mult)
                v.drain()
                v.tensor_sub(stat[:, 3:4], stat[:, 1:2], stat[:, 2:3])
                v.drain()
                v.tensor_scalar(out=stat[:, 4:5], in0=stat[:, 3:4], scalar1=LN_EPS,
                                scalar2=None, op0=OP.add)
                v.tensor_scalar(out=stat[:, 5:6], in0=stat[:, 3:4],
                                scalar1=1.0 + RMS_EPS, scalar2=LN_EPS * RMS_EPS,
                                op0=OP.mult, op1=OP.add)
                v.drain().then_inc(s_v.h, 1)
            ve(f)

            nsq = s_s.inc()
            def f(s, nst=nst):
                s.wait_ge(s_v.h, nst)
                s.activation(out=stat[:, 6:8], in_=stat[:, 4:6], func=AF.Sqrt)
                s.drain().then_inc(s_s.h, 1)
            sa(f)

            nsb = s_v.inc()
            def f(v, nsq=nsq, t=t):
                v.wait_ge(s_s.h, nsq)
                v.reciprocal(stat[:, 8:10], stat[:, 6:8])
                v.drain()
                v.tensor_tensor(out=rinv[:, t:t + 1], in0=stat[:, 8:9], in1=stat[:, 7:8],
                                op=OP.mult)
                v.drain()
                v.tensor_scalar(out=stat[:, 10:11], in0=stat[:, 0:1],
                                scalar1=stat[:, 9:10], scalar2=-1.0,
                                op0=OP.mult, op1=OP.mult)
                v.drain().then_inc(s_v.h, 1)
            ve(f)

            nxn = s_s.inc()
            xn2_done[t] = nxn
            def f(s, xt=xt, x2t=x2t, nsb=nsb):
                s.wait_ge(s_v.h, nsb)
                s.activation(out=x2t, in_=xt, func=AF.Identity, bias=stat[:, 10:11],
                             scale=stat[:, 9:10])
                s.drain().then_inc(s_s.h, 1)
            sa(f)

            ntp = s_t.inc()
            def f(t_, x2t=x2t, nxn=nxn, need_ps=prev_evac, t=t, p0c=p0c):
                t_.wait_ge(s_s.h, nxn)
                if t == 0:
                    t_.wait_ge(s_c.h, p0c)
                if need_ps:
                    t_.wait_ge(s_v.h, need_ps)
                mm = None
                for ch in range(NCH):
                    mm = t_.transpose(PSB[0][:, ch * 128:(ch + 1) * 128],
                                      x2t[:, ch * 128:(ch + 1) * 128], ident)
                mm.then_inc(s_t.h, 1)
            te(f)

            nev = s_v.inc()
            def f(v, t=t, ntp=ntp):
                v.wait_ge(s_t.h, ntp)
                for ch in range(NCH):
                    v.tensor_copy(xn2fem[:, ch * E + t * 128:ch * E + (t + 1) * 128],
                                  PSB[0][:, ch * 128:(ch + 1) * 128])
                v.engine_nop()
                v.drain().then_inc(s_v.h, 1)
            ve(f)
            prev_evac = nev
        BK["p1_v"] = s_v.n
        BK["p1_s"] = s_s.n
        dotap("xn2fem", xn2fem, s_v, s_v.n)
        dotap("rinv", rinv[:, 0:NT_E], s_v, s_v.n)

    def p2():
        p1v = BK["p1_v"]
        gate_ev = []
        for qb in range(R // 512):
            bank = qb % 2
            ntm = s_t.inc()
            guard = gate_ev[-2] if len(gate_ev) >= 2 else 0
            def f(t, qb=qb, bank=bank, p1v=p1v, guard=guard):
                if qb == 0:
                    t.wait_ge(s_v.h, p1v)
                    t.wait_ge(s_c.h, BK["p0_c"])
                if guard:
                    t.wait_ge(s_s.h, guard)
                mm = None
                for ch in range(NCH):
                    mm = t.matmul(PS[bank][0:HATT, 0:512],
                                  wg_sb[:, ch * HATT:(ch + 1) * HATT],
                                  xn2fem[:, ch * E + qb * 512:ch * E + (qb + 1) * 512],
                                  start=(ch == 0), stop=(ch == NCH - 1))
                mm.then_inc(s_t.h, 1)
            te(f)
            nsg = s_s.inc()
            gate_ev.append(nsg)
            def f(s, qb=qb, bank=bank, ntm=ntm):
                s.wait_ge(s_t.h, ntm)
                s.activation(out=gstg[0:HATT, qb * 512:(qb + 1) * 512],
                             in_=PS[bank][0:HATT, 0:512], func=AF.Sigmoid)
                s.drain().then_inc(s_s.h, 1)
            sa(f)
        BK["gates_s"] = s_s.n
        dotap("gates", gstg[:, 0:R], s_s, s_s.n)

        stream_block.piece = 0
        stream_block.prev = [BK["p0_c"], BK["p0_c"]]
        gsn = BK["gates_s"]
        for h in range(HFFN):
            wfr = stream_block(wf_in, wfp, 0, NCH, h * 256, 256,
                               extra_s=BK.get("p2_ffq_last", 0))
            for gi in range(NQS):
                ntm = s_t.inc()
                bank_guard = BK.get("p2_ffq_last", 0)
                def f(t, h=h, gi=gi, wfr=wfr, gsn=gsn, bank_guard=bank_guard):
                    if gi == 0:
                        t.wait_ge(s_c.h, wfr)
                    t.wait_ge(s_s.h, max(gsn, bank_guard))
                    mm = None
                    for half in range(2):
                        for ch in range(NCH):
                            mm = t.matmul(PS[half][:, 0:512],
                                     wfp[:, ch * 256 + half * 128:ch * 256 + half * 128 + 128],
                                     xn2fem[:, ch * E + gi * 512:ch * E + (gi + 1) * 512],
                                     start=(ch == 0), stop=(ch == NCH - 1))
                    mm.then_inc(s_t.h, 1)
                te(f)
                nev = s_s.inc(2)
                def f(s, ntm=ntm):
                    s.wait_ge(s_t.h, ntm)
                    s.copy(ffnq[:, 0:512], PS[0][:, 0:512])
                    s.drain().then_inc(s_s.h, 1)
                    s.copy(ffnq[:, 512:1024], PS[1][:, 0:512])
                    s.drain().then_inc(s_s.h, 1)
                sa(f)
                BK["p2_ffq_last"] = nev
                if gi == 0 and h == 0:
                    dotap("ffnq", ffnq, s_s, s_s.n)
                    BK["p2_ffq_last"] = max(BK["p2_ffq_last"], 0)

                for tk_ in range(4):
                    t_glob = gi * 4 + tk_
                    p2_tile_head(tk_, t_glob, h)
        BK["p2_v"] = s_v.n
        dotap("ei", ei_all, s_v, s_v.n)
        dotap("sc", sc_all, s_v, s_v.n)

    def p2_tile_head(tk_, t_glob, h):
        ffq_evac = BK["p2_ffq_last"]
        fix_prev = BK.get("p2_fix_prev", 0)
        ntm = s_t.inc()
        def f(t, h=h, tk_=tk_, ffq_evac=ffq_evac, fix_prev=fix_prev):
            t.wait_ge(s_s.h, ffq_evac)
            if fix_prev:
                t.wait_ge(s_v.h, fix_prev)
            t.matmul(PS[2][:, 0:NS],
                     ffnq[:, tk_ * 128:(tk_ + 1) * 128],
                     k1t, start=True, stop=True)
            t.matmul(PS[2][:, NS:2 * NS],
                     ffnq[:, 512 + tk_ * 128:512 + (tk_ + 1) * 128],
                     k2t, start=True, stop=True).then_inc(s_t.h, 1)
        te(f)

        nfix = s_v.inc()
        BK["p2_fix_prev"] = nfix
        def f(v, t_glob=t_glob, h=h, ntm=ntm):
            v.wait_ge(s_t.h, ntm)
            v.wait_ge(s_c.h, BK["rep_c1_c"])
            v.scalar_tensor_tensor(out=s12, in0=PS[2][:, 0:2 * NS],
                                   scalar=rinv[:, t_glob:t_glob + 1],
                                   in1=rep_c1[:, h * 2 * NS:(h + 1) * 2 * NS],
                                   op0=OP.mult, op1=OP.add)
            v.drain().then_inc(s_v.h, 1)
        ve(f)
        if t_glob == 0 and h == 0:
            dotap("s12", s12, s_v, s_v.n)

        ntop = s_v.inc()
        def f(v, t_glob=t_glob, h=h):
            src0 = s12[:, 0:NS]
            src1 = s12[:, NS:2 * NS]
            rep0 = srep[:, 0:NS]
            rep1 = srep[:, NS:2 * NS]
            vv0 = topv[:, 0:16]
            vv1 = topv[:, 16:32]
            ii0 = topi[:, 0:16]
            ii1 = topi[:, 16:32]
            v.max(out=vv0[:, 0:8], in_=src0)
            v.max(out=vv1[:, 0:8], in_=src1)
            v.drain()
            v.max_index(out=ii0[:, 0:8], in_max=vv0[:, 0:8], in_values=src0)
            v.max_index(out=ii1[:, 0:8], in_max=vv1[:, 0:8], in_values=src1)
            v.drain()
            v.match_replace(out=rep0, in_to_replace=vv0[:, 0:8], in_values=src0,
                            imm_value=NEG)
            v.match_replace(out=rep1, in_to_replace=vv1[:, 0:8], in_values=src1,
                            imm_value=NEG)
            v.drain()
            v.max(out=vv0[:, 8:16], in_=rep0)
            v.max(out=vv1[:, 8:16], in_=rep1)
            v.drain()
            v.max_index(out=ii0[:, 8:16], in_max=vv0[:, 8:16], in_values=rep0)
            v.max_index(out=ii1[:, 8:16], in_max=vv1[:, 8:16], in_values=rep1)
            v.memset(comb, NEG)
            v.drain()
            v.tensor_tensor(
                out=comb.rearrange("p (j l) -> p j l", l=16)[:, 0:10, 0:10],
                in0=topv[:, 0:10].unsqueeze(-1).broadcast_to([128, 10, 10]),
                in1=topv[:, 16:26].unsqueeze(1).broadcast_to([128, 10, 10]),
                op=OP.add)
            v.drain()
            v.max(out=cv8[:, 0:8], in_=comb)
            v.drain()
            v.match_replace(out=crep, in_to_replace=cv8[:, 0:8], in_values=comb,
                            imm_value=NEG)
            v.max_index(out=ci8[:, 0:8], in_max=cv8[:, 0:8], in_values=comb)
            v.drain()
            v.max(out=cv8[:, 8:16], in_=crep)
            v.drain()
            v.max_index(out=ci8[:, 8:16], in_max=cv8[:, 8:16], in_values=crep)
            v.drain()
            v.tensor_copy(bji[:, 0:8], ci8[:, 0:8])
            v.tensor_copy(bji[:, 8:10], ci8[:, 8:10])
            v.tensor_copy(i1f[:, 0:10], topi[:, 0:10])
            v.tensor_copy(i1f[:, 16:26], topi[:, 16:26])
            v.reduce_max(stat[:, 14:15], cv8[:, 0:10], axis=AX.X)
            v.drain()
            v.tensor_scalar(out=ci8[:, 0:10], in0=bji[:, 0:10], scalar1=4,
                            scalar2=None, op0=OP.logical_shift_right)
            v.tensor_scalar(out=valst.bitcast(U32)[:, 0:10], in0=bji[:, 0:10],
                            scalar1=15, scalar2=None, op0=OP.bitwise_and)
            v.tensor_scalar(out=stat[:, 15:16], in0=stat[:, 14:15], scalar1=-1.0,
                            scalar2=None, op0=OP.mult)
            v.drain()
            v.tensor_copy(bjf[:, 0:10], ci8[:, 0:10])
            v.tensor_copy(bjf[:, 10:20], valst.bitcast(U32)[:, 0:10])
            v.drain()
            mmA = mskt[:, 0:100].rearrange("p (k j) -> p k j", j=10)
            mmB = mskt[:, 100:200].rearrange("p (k j) -> p k j", j=10)
            v.tensor_tensor(
                out=mmA,
                in0=bjf[:, 0:10].unsqueeze(-1).broadcast_to([128, 10, 10]),
                in1=iota10.rearrange("p (k j) -> p k j", j=10), op=OP.is_equal)
            v.tensor_tensor(
                out=mmB,
                in0=bjf[:, 10:20].unsqueeze(-1).broadcast_to([128, 10, 10]),
                in1=iota10.rearrange("p (k j) -> p k j", j=10), op=OP.is_equal)
            v.drain()
            v.tensor_tensor(
                out=mmA, in0=mmA,
                in1=i1f[:, 0:10].unsqueeze(1).broadcast_to([128, 10, 10]),
                op=OP.mult)
            v.tensor_tensor(
                out=mmB, in0=mmB,
                in1=i1f[:, 16:26].unsqueeze(1).broadcast_to([128, 10, 10]),
                op=OP.mult)
            v.drain()
            v.tensor_reduce(out=e12[:, 0:10], in_=mmA, axis=AX.X, op=OP.add)
            v.tensor_reduce(out=e12[:, 10:20], in_=mmB, axis=AX.X, op=OP.add)
            v.drain()
            v.scalar_tensor_tensor(out=eff[:, 0:10], in0=e12[:, 0:10],
                                   scalar=float(NS), in1=e12[:, 10:20],
                                   op0=OP.mult, op1=OP.add)
            v.drain()
            v.tensor_copy(ei_all[:, (t_glob * HFFN + h) * 10:(t_glob * HFFN + h) * 10 + 10],
                          eff[:, 0:10])
            v.drain().then_inc(s_v.h, 1)
        ve(f)

        nse = s_s.inc()
        def f(s, ntop=ntop):
            s.wait_ge(s_v.h, ntop)
            s.activation(out=pval[:, 0:10], in_=cv8[:, 0:10], func=AF.Exp,
                         bias=stat[:, 15:16], accum_out=stat[:, 13:14])
            s.drain().then_inc(s_s.h, 1)
        sa(f)
        nsc = s_v.inc()
        def f(v, t_glob=t_glob, h=h, nse=nse):
            v.wait_ge(s_s.h, nse)
            v.reciprocal(stat[:, 12:13], stat[:, 13:14])
            v.drain()
            v.tensor_scalar(out=sc_all[:, (t_glob * HFFN + h) * 10:(t_glob * HFFN + h) * 10 + 10],
                            in0=pval[:, 0:10], scalar1=stat[:, 12:13], scalar2=None,
                            op0=OP.mult)
            v.drain().then_inc(s_v.h, 1)
        ve(f)

    def p3():
        unit = 0
        um_done = {}
        p2v = BK["p2_v"]
        for t in range(NT_R):
            buf = t % 2
            xt = xn_tok[:, buf * D:(buf + 1) * D]
            prev_ev = BK.get("p3_xnev", 0)
            ntp = s_t.inc()
            def f(t_, t=t, prev_ev=prev_ev, p2v=p2v):
                if t == 0:
                    t_.wait_ge(s_v.h, p2v)
                if prev_ev:
                    t_.wait_ge(s_v.h, prev_ev)
                mm = None
                for ch in range(NCH):
                    mm = t_.transpose(PSB[1][:, ch * 128:(ch + 1) * 128],
                                      xn2fem[:, ch * E + t * 128:ch * E + (t + 1) * 128],
                                      ident)
                mm.then_inc(s_t.h, 1)
            te(f)
            nxe = s_v.inc()
            BK["p3_xnev"] = nxe
            def f(v, xt=xt, t=t, ntp=ntp):
                v.wait_ge(s_t.h, ntp)
                for ch in range(NCH):
                    v.tensor_scalar(out=xt[:, ch * 128:(ch + 1) * 128],
                                    in0=PSB[1][:, ch * 128:(ch + 1) * 128],
                                    scalar1=rinv[:, t:t + 1], scalar2=None, op0=OP.mult)
                v.engine_nop()
                v.drain().then_inc(s_v.h, 1)
            ve(f)
            if t == 0 and "hcol" in tap_d:
                pass

            silu_prev = BK.get("p3_silu", 0)
            for h in range(HFFN):
                for kp in range(5):
                    u0 = unit
                    pu = []
                    for dk in range(2):
                        u = u0 + dk
                        k = 2 * kp + dk
                        sl = u % NGP
                        gd_t = gbuf[:, (2 * sl) * D:(2 * sl + 1) * D]
                        gu_t = gbuf[:, (2 * sl + 1) * D:(2 * sl + 2) * D]
                        need_gc = max(0, u - NGP + 1)
                        need_um = um_done.get(u - NGP, 0)
                        ngdn = s_gdn[sl].inc(16)
                        ngup = s_gup[sl].inc(16)
                        idx_col = ei_all[:, (t * HFFN + h) * 10 + k:(t * HFFN + h) * 10 + k + 1]
                        def f(g, gd_t=gd_t, gu_t=gu_t, idx_col=idx_col, need_gc=need_gc,
                              need_um=need_um, p2v=p2v, u=u, sl=sl):
                            if u == 0:
                                g.wait_ge(s_v.h, p2v)
                            if need_gc:
                                g.wait_ge(s_gc.h, need_gc)
                            if need_um:
                                g.wait_ge(s_um.h, need_um)
                            g.indirect_dma_start(out=gd_t, out_offset=None, in_=wdn_in,
                                                 in_offset=IndirectOffsetOnAxis(ap=idx_col, axis=0)) \
                             .then_inc(s_gdn[sl].h, 16)
                            g.indirect_dma_start(out=gu_t, out_offset=None, in_=wup_in,
                                                 in_offset=IndirectOffsetOnAxis(ap=idx_col, axis=0)) \
                             .then_inc(s_gup[sl].h, 16)
                        gp(f)
                        pu.append((u, sl, ngdn, ngup, gd_t, gu_t))

                    kk0 = u0 % 16
                    ngc = s_gc.inc(2)
                    def f(v, pu=pu, xt=xt):
                        for (u_, sl_, ngdn_, ngup_, gd_, gu_) in pu:
                            v.wait_ge(s_gdn[sl_].h, ngdn_)
                            v.scalar_tensor_tensor(out=dmp3, in0=gd_, scalar=1.0, in1=xt,
                                                   op0=OP.mult, op1=OP.mult,
                                                   accum_out=hcol[:, (u_ % 16):(u_ % 16) + 1])
                        v.drain().then_inc(s_gc.h, 2)
                    ve(f)

                    nge = s_s.inc()
                    def f(s, kk0=kk0, ngc=ngc):
                        s.wait_ge(s_gc.h, ngc)
                        s.activation(out=hgel[:, kk0:kk0 + 2], in_=hcol[:, kk0:kk0 + 2],
                                     func=AF.Gelu)
                        s.drain().then_inc(s_s.h, 1)
                    sa(f)

                    need_um2 = um_done.get(u0 - 1, 0)
                    ndg = s_dg.inc()
                    def f(v, t=t, h=h, kp=kp, u0=u0, nge=nge, need_um2=need_um2):
                        v.wait_ge(s_s.h, nge)
                        if need_um2:
                            v.wait_ge(s_um.h, need_um2)
                        for dk in range(2):
                            u = u0 + dk
                            k = 2 * kp + dk
                            v.tensor_scalar(out=diag[:, (u % 2) * 128:(u % 2) * 128 + 128],
                                            in0=ident, scalar1=hgel[:, (u % 16):(u % 16) + 1],
                                            scalar2=sc_all[:, (t * HFFN + h) * 10 + k:
                                                           (t * HFFN + h) * 10 + k + 1],
                                            op0=OP.mult, op1=OP.mult)
                        v.drain().then_inc(s_dg.h, 1)
                    ve(f)

                    for dk in range(2):
                        u = u0 + dk
                        k = 2 * kp + dk
                        hk = h * 10 + k
                        num = s_um.inc()
                        um_done[u] = num
                        ngup = pu[dk][3]
                        def f(t_, u=u, hk=hk, ndg=ndg, silu_prev=silu_prev, ngup=ngup):
                            t_.wait_ge(s_dg.h, ndg)
                            sl = u % NGP
                            t_.wait_ge(s_gup[sl].h, ngup)
                            if hk == 0 and silu_prev:
                                t_.wait_ge(s_s.h, silu_prev)
                            gu_t = gbuf[:, (2 * sl + 1) * D:(2 * sl + 2) * D]
                            dg_t = diag[:, (u % 2) * 128:(u % 2) * 128 + 128]
                            mm = None
                            for (bi, c0, cw) in DSPL:
                                mm = t_.matmul(PS[3 + bi][:, 0:cw], dg_t, gu_t[:, c0:c0 + cw],
                                          start=(hk == 0), stop=(hk == HFFN * 10 - 1))
                            mm.then_inc(s_um.h, 1)
                        te(f)
                        unit += 1

            nsl = s_s.inc(len(DSPL))
            BK["p3_silu"] = nsl
            ulast = s_um.n
            def f(s, t=t, ulast=ulast):
                s.wait_ge(s_um.h, ulast)
                for (bi, c0, cw) in DSPL:
                    s.activation(out=g_out[:, t * D + c0:t * D + c0 + cw],
                                 in_=PS[3 + bi][:, 0:cw], func=AF.Silu)
                s.drain().then_inc(s_s.h, len(DSPL))
            sa(f)
        BK["p3_s"] = s_s.n
        dotap("gff", g_out, s_s, s_s.n)

    def stream_block(w_in, dst, rows0, nch, c0, cw, extra_s=0, extra_v=0,
                     fold=False, scale=1.0):
        for ch in range(nch):
            buf = stream_block.piece % 2
            st_ap = wld[:, buf * 512:buf * 512 + cw]
            need = stream_block.prev[buf]
            sw = s_wp[buf]
            ndma = sw.inc(16)
            def f(e, st_ap=st_ap, ch=ch, need=need, w_in=w_in, rows0=rows0, c0=c0,
                  cw=cw, extra_s=extra_s, extra_v=extra_v, sw=sw):
                e.wait_ge(s_c.h, need)
                if extra_s and ch == 0:
                    e.wait_ge(s_s.h, extra_s)
                if extra_v and ch == 0:
                    e.wait_ge(s_v.h, extra_v)
                e.dma_start(st_ap, w_in[rows0 + ch * 128:rows0 + ch * 128 + 128,
                                        c0:c0 + cw]).then_inc(sw.h, 16)
            sy(f)
            ncst = s_c.inc()
            def f(v, st_ap=st_ap, ch=ch, ndma=ndma, dst=dst, cw=cw, fold=fold,
                  scale=scale, sw=sw):
                v.wait_ge(sw.h, ndma)
                if fold:
                    v.tensor_scalar(out=dst[:, ch * cw:(ch + 1) * cw], in0=st_ap,
                                    scalar1=nw_f[:, ch:ch + 1], scalar2=float(scale),
                                    op0=OP.mult, op1=OP.mult)
                    v.drain().then_inc(s_c.h, 1)
                else:
                    v.tensor_copy(dst[:, ch * cw:(ch + 1) * cw], st_ap)
                    v.drain().then_inc(s_c.h, 1)
            ve(f)
            stream_block.prev[buf] = ncst
            stream_block.piece += 1
        return s_c.n

    def p4():
        stream_block.piece = 0
        stream_block.prev = [BK["p0_c"], BK["p0_c"]]
        p1v = BK["p1_v"]
        p3s = BK["p3_s"]
        BK["p4_glstg_hist"] = []
        BK["p4_unit"] = 0

        for hp in range(NPAIR):
            h0 = 2 * hp
            wkr = stream_block(wkv_in, wkvq_a, 0, NCH, h0 * DH, 128,
                               extra_s=(p3s if hp == 0 else 0),
                               extra_v=BK.get("p4_vev", 0), fold=True)
            prev_kev = BK.get("p4_kev", 0)
            ssn = s_s.n
            kev_hist = []
            for nb in range(E // 512):
                bank = 5 + nb % 2
                ntm = s_t.inc()
                guard = kev_hist[-2] if len(kev_hist) >= 2 else 0
                def f(t, nb=nb, bank=bank, wkr=wkr, prev_kev=prev_kev, p1v=p1v,
                      ssn=ssn, guard=guard):
                    if nb == 0:
                        t.wait_ge(s_c.h, wkr)
                        t.wait_ge(s_v.h, max(p1v, prev_kev))
                        t.wait_ge(s_s.h, ssn)
                    if guard:
                        t.wait_ge(s_v.h, guard)
                    mm = None
                    for ch in range(NCH):
                        mm = t.matmul(PS[bank][:, 0:512], wkvq_a[:, ch * 128:(ch + 1) * 128],
                                 xn2fem[:, ch * E + nb * 512:ch * E + (nb + 1) * 512],
                                 start=(ch == 0), stop=(ch == NCH - 1))
                    mm.then_inc(s_t.h, 1)
                te(f)
                nev = s_v.inc()
                kev_hist.append(nev)
                def f(v, nb=nb, bank=bank, ntm=ntm):
                    v.wait_ge(s_t.h, ntm)
                    v.tensor_copy(kpair[:, nb * 512:(nb + 1) * 512], PS[bank][:, 0:512])
                    v.drain().then_inc(s_v.h, 1)
                ve(f)
            if hp == 0:
                dotap("kpair", kpair, s_v, s_v.n)
            BK["p4_kev"] = s_v.n

            wqr = stream_block(wq_in, wkvq_a, 0, NCH, h0 * DH, 128,
                               extra_v=BK["p4_kev"], fold=True, scale=0.125)
            prev_qev = BK.get("p4_qev", 0)
            ssn = s_s.n
            qev_hist = []
            for nb in range(R // 512):
                bank = 5 + nb % 2
                ntm = s_t.inc()
                guard = qev_hist[-2] if len(qev_hist) >= 2 else 0
                def f(t, nb=nb, bank=bank, wqr=wqr, prev_qev=prev_qev, ssn=ssn,
                      guard=guard):
                    if nb == 0:
                        t.wait_ge(s_c.h, wqr)
                        if prev_qev:
                            t.wait_ge(s_v.h, prev_qev)
                        t.wait_ge(s_s.h, ssn)
                    if guard:
                        t.wait_ge(s_v.h, guard)
                    mm = None
                    for ch in range(NCH):
                        mm = t.matmul(PS[bank][:, 0:512], wkvq_a[:, ch * 128:(ch + 1) * 128],
                                 xn2fem[:, ch * E + nb * 512:ch * E + (nb + 1) * 512],
                                 start=(ch == 0), stop=(ch == NCH - 1))
                    mm.then_inc(s_t.h, 1)
                te(f)
                nev = s_v.inc()
                qev_hist.append(nev)
                def f(v, nb=nb, bank=bank, ntm=ntm):
                    v.wait_ge(s_t.h, ntm)
                    v.tensor_copy(qpair[:, nb * 512:(nb + 1) * 512], PS[bank][:, 0:512])
                    v.drain().then_inc(s_v.h, 1)
                ve(f)
            BK["p4_qev"] = s_v.n

            wvr = stream_block(wkv_in, wkvq_a, 0, NCH, DI + h0 * DH, 128,
                               extra_v=BK["p4_qev"], fold=True)
            prev_vev = BK.get("p4_vev", 0)
            ssn = s_s.n
            vev_hist = []
            for tt in range(NT_E):
                bank = 5 + tt % 2
                ntm = s_t.inc()
                guard = vev_hist[-2] if len(vev_hist) >= 2 else 0
                def f(t, tt=tt, bank=bank, wvr=wvr, prev_vev=prev_vev, ssn=ssn,
                      guard=guard):
                    if tt == 0:
                        t.wait_ge(s_c.h, wvr)
                        if prev_vev:
                            t.wait_ge(s_v.h, prev_vev)
                        t.wait_ge(s_s.h, ssn)
                    if guard:
                        t.wait_ge(s_v.h, guard)
                    mm = None
                    for ch in range(NCH):
                        mm = t.matmul(PS[bank][:, 0:128],
                                 xn2fem[:, ch * E + tt * 128:ch * E + (tt + 1) * 128],
                                 wkvq_a[:, ch * 128:(ch + 1) * 128],
                                 start=(ch == 0), stop=(ch == NCH - 1))
                    mm.then_inc(s_t.h, 1)
                te(f)
                nev = s_v.inc()
                vev_hist.append(nev)
                def f(v, tt=tt, bank=bank, ntm=ntm):
                    v.wait_ge(s_t.h, ntm)
                    for hh in range(2):
                        v.tensor_copy(v2[:, (hh * NT_E + tt) * 64:(hh * NT_E + tt) * 64 + 64],
                                      PS[bank][:, hh * 64:(hh + 1) * 64])
                    v.engine_nop()
                    v.drain().then_inc(s_v.h, 1)
                ve(f)
            BK["p4_vev"] = s_v.n

            for hh in range(2):
                h = 2 * hp + hh
                pe = 64 * hh
                for qs in range(NQS):
                    p4_head_qs(hp, hh, h, pe, qs)
        BK["p4_fz"] = s_fz.n
        dotap("gatedT", gatedT, s_fz, s_fz.n)

    def p4_head_qs(hp, hh, h, pe, qs):
        gqi = len(BK["p4_glstg_hist"])
        buf = gqi % 2
        sgp = s_sgp[buf]
        gmv = sgp.inc(16)
        gsr = max(BK["gates_s"], BK["p3_s"] if gqi == 0 else 0)
        guard = BK["p4_glstg_hist"][-2] if len(BK["p4_glstg_hist"]) >= 2 else 0
        def f(e, h=h, pe=pe, qs=qs, gsr=gsr, buf=buf, guard=guard, sgp=sgp):
            e.wait_ge(s_s.h, gsr)
            if guard:
                e.wait_ge(s_fz.h, guard)
            e.dma_start(glstg[pe:pe + 1, buf * 512:(buf + 1) * 512],
                        gstg[h:h + 1, qs * 512:(qs + 1) * 512]).then_inc(sgp.h, 16)
        sy(f)

        units = []
        for j in range(4 * qs):
            units.append((j, None))
        for a in range(4):
            units.append((4 * qs + a, ("A", a)))
        for j in range(4 * qs):
            units.append((NT_R + j, None))
        for a in range(4):
            units.append((NT_R + 4 * qs + a, ("B", a)))

        kevn = BK["p4_kev"]; qevn = BK["p4_qev"]; vevn = BK["p4_vev"]
        fz_prev = BK.get("p4_fz_prev", 0)
        exp_done = {}
        ptm_done = {}
        av_done = {}
        BANKS4 = (5, 6, 0, 1)
        for qi4 in range(len(units) // 4):
            tms = []
            for dk in range(4):
                ui = 4 * qi4 + dk
                kt, mk = units[ui]
                sbank = BANKS4[ui % 4]
                need_exp = exp_done.get(ui - 4, 0)
                ntm = s_t.inc()
                def f(t, kt=kt, sbank=sbank, need_exp=need_exp, kevn=kevn, qevn=qevn,
                      vevn=vevn, qs=qs, hh=hh, ui=ui, fz_prev=fz_prev):
                    if ui == 0:
                        t.wait_ge(s_v.h, max(kevn, qevn, vevn))
                        if fz_prev:
                            t.wait_ge(s_fz.h, fz_prev)
                    if need_exp:
                        t.wait_ge(s_s.h, need_exp)
                    t.matmul(PS[sbank][:, 0:512],
                             kpair[64 * hh:64 * hh + 64, kt * 128:(kt + 1) * 128],
                             qpair[64 * hh:64 * hh + 64, qs * 512:(qs + 1) * 512],
                             start=True, stop=True,
                             tile_position=(64 * hh, 0) if hh else None) \
                     .then_inc(s_t.h, 1)
                te(f)
                tms.append(ntm)

            uo = 4 * qi4 + 3
            need_av = av_done.get(uo - NPT, 0)
            nse = s_s.inc(4)
            for dk in range(4):
                exp_done[4 * qi4 + dk] = nse
            def f(s, ntm_=tms[3], need_av=need_av, uo=uo):
                s.wait_ge(s_t.h, ntm_)
                if need_av:
                    s.wait_ge(s_av.h, need_av)
                for uj in range(uo - 3, uo + 1):
                    s.activation(out=PTS[uj % NPT],
                                 in_=PS[BANKS4[uj % 4]][:, 0:512], func=AF.Exp,
                                 bias=cexp[:, 0:1])
                s.drain().then_inc(s_s.h, 4)
            sa(f)

            mks = [units[4 * qi4 + dk][1] for dk in range(4)]
            assert all((m is None) == (mks[0] is None) for m in mks)
            if mks[0] is not None:
                npm = s_v.inc()
                def f(v, nse=nse, qi4=qi4, mks=mks):
                    v.wait_ge(s_s.h, nse)
                    for dk in range(4):
                        ui = 4 * qi4 + dk
                        kind, a = mks[dk]
                        ps_ = PTS[ui % NPT]
                        if a:
                            v.memset(ps_[:, 0:128 * a], 0.0)
                        blk = ps_[:, 128 * a:128 * (a + 1)]
                        if kind == "A":
                            v.tensor_tensor(out=blk, in0=blk, in1=tril, op=OP.mult)
                        else:
                            v.tensor_scalar(out=blk, in0=blk, scalar1=cfl[:, 0:1],
                                            scalar2=None, op0=OP.mult)
                    v.drain().then_inc(s_v.h, 1)
                ve(f)
                for dk in range(4):
                    ptm_done[4 * qi4 + dk] = ("v", npm)
            else:
                for dk in range(4):
                    ptm_done[4 * qi4 + dk] = ("s", nse)

            for dk in range(4):
                ui = 4 * qi4 + dk
                kt, mk = units[ui]
                slot = ui % NPT
                nav = s_av.inc()
                av_done[ui] = nav
                eng, val = ptm_done[ui]
                last = (ui == len(units) - 1)
                def f(t, kt=kt, slot=slot, hh=hh, pe=pe, ui=ui, eng=eng, val=val, last=last):
                    if eng == "s":
                        t.wait_ge(s_s.h, val)
                    else:
                        t.wait_ge(s_v.h, val)
                    vt = v2[:, (hh * NT_E + kt) * 64:(hh * NT_E + kt) * 64 + 64]
                    pts = PTS[slot]
                    if pe == 0:
                        t.matmul(PS[7][0:64, 0:512], vt, pts, start=(ui == 0), stop=last)
                        t.matmul(PS[2][0:1, 0:512], onescol, pts, start=(ui == 0),
                                 stop=last).then_inc(s_av.h, 1)
                    else:
                        t.matmul(PS[7][64:128, 0:512], vt, pts, start=(ui == 0), stop=last,
                                 tile_position=(0, 64))
                        t.matmul(PS[2][64:65, 0:512], onescol, pts, start=(ui == 0),
                                 stop=last, tile_position=(0, 64)).then_inc(s_av.h, 1)
                te(f)

        navl = s_av.n
        nfz1 = s_v.inc()
        def f(v, pe=pe, navl=navl, gmv=gmv, buf=buf, sgp=sgp):
            v.wait_ge(s_av.h, navl)
            v.wait_ge(sgp.h, gmv)
            v.reciprocal(lrow[pe:pe + 1, 0:512], PS[2][pe:pe + 1, 0:512])
            v.drain()
            v.tensor_tensor(out=glrow[pe:pe + 1, 0:512], in0=lrow[pe:pe + 1, 0:512],
                            in1=glstg[pe:pe + 1, buf * 512:(buf + 1) * 512],
                            op=OP.mult)
            v.drain().then_inc(s_v.h, 1)
        ve(f)
        ntr = s_t.inc()
        def f(t, pe=pe, nfz1=nfz1):
            t.wait_ge(s_v.h, nfz1)
            if pe == 0:
                t.matmul(PS[2][0:64, 0:512], ones_bf[0:1, 0:64], glrow[0:1, 0:512],
                         start=True, stop=True).then_inc(s_t.h, 1)
            else:
                t.matmul(PS[2][64:128, 0:512], ones_bf[64:65, 0:64],
                         glrow[64:65, 0:512], start=True, stop=True,
                         tile_position=(64, 64)).then_inc(s_t.h, 1)
        te(f)
        nfe = s_s.inc()
        def f(s, pe=pe, ntr=ntr):
            s.wait_ge(s_t.h, ntr)
            s.copy(fin[pe:pe + 64, 0:512], PS[2][pe:pe + 64, 0:512])
            s.drain().then_inc(s_s.h, 1)
        sa(f)
        nfz = s_fz.inc()
        BK["p4_fz_prev"] = nfz
        BK["p4_glstg_hist"].append(nfz)
        def f(v, pe=pe, hp=hp, qs=qs, nfe=nfe):
            v.wait_ge(s_s.h, nfe)
            v.tensor_tensor(out=gatedT[pe:pe + 64, hp * R + qs * 512:hp * R + (qs + 1) * 512],
                            in0=PS[7][pe:pe + 64, 0:512], in1=fin[pe:pe + 64, 0:512],
                            op=OP.mult)
            v.drain().then_inc(s_fz.h, 1)
        ve(f)

    def p5():
        stream_block.piece = 0
        stream_block.prev = [s_c.n, s_c.n]
        p4fz = BK["p4_fz"]
        first = True
        for ch in range(NDI):
            for (_, c0, cw) in DSPL:
                buf = stream_block.piece % 2
                st_ap = wld[:, buf * 512:buf * 512 + cw]
                need = stream_block.prev[buf]
                sw = s_wp[buf]
                ndma = sw.inc(16)
                def f(e, st_ap=st_ap, ch=ch, c0=c0, cw=cw, need=need, first=first,
                      p4fz=p4fz, sw=sw):
                    e.wait_ge(s_c.h, need)
                    if first:
                        e.wait_ge(s_fz.h, p4fz)
                    e.dma_start(st_ap, wout_in[ch * 128:ch * 128 + 128, c0:c0 + cw]) \
                     .then_inc(sw.h, 16)
                sy(f)
                ncst = s_c.inc()
                def f(v, st_ap=st_ap, ch=ch, c0=c0, cw=cw, ndma=ndma, sw=sw):
                    v.wait_ge(sw.h, ndma)
                    v.tensor_copy(wout_a[:, ch * D + c0:ch * D + c0 + cw], st_ap)
                    v.drain().then_inc(s_c.h, 1)
                ve(f)
                stream_block.prev[buf] = ncst
                stream_block.piece += 1
                first = False
        wor = s_c.n
        p4fzn = BK["p4_fz"]

        ep_done = {}
        for t in range(NT_R):
            buf = t % 2
            xt = xload[:, buf * D:(buf + 1) * D]
            need = ep_done.get(t - 2, 0)
            sx = s_xp[buf]
            ndma = sx.inc(16)
            def f(e, xt=xt, t=t, need=need, p4fzn=p4fzn, sx=sx):
                if t == 0:
                    e.wait_ge(s_fz.h, p4fzn)
                if need:
                    e.wait_ge(s_v.h, need)
                e.dma_start(xt, xe[t * 128:(t + 1) * 128, :]).then_inc(sx.h, 16)
            sy(f)

            ntm = s_t.inc()
            ep_prev = BK.get("p5_ep_prev", 0)
            def f(t_, t=t, wor=wor, p4fzn=p4fzn, ep_prev=ep_prev):
                if t == 0:
                    t_.wait_ge(s_c.h, wor)
                    t_.wait_ge(s_fz.h, p4fzn)
                if ep_prev:
                    t_.wait_ge(s_v.h, ep_prev)
                mm = None
                for pr in range(NPAIR):
                    for (bi, c0, cw) in DSPL:
                        mm = t_.matmul(PS[bi][:, 0:cw],
                                  gatedT[:, pr * R + t * 128:pr * R + (t + 1) * 128],
                                  wout_a[:, pr * D + c0:pr * D + c0 + cw],
                                  start=(pr == 0), stop=(pr == NPAIR - 1))
                mm.then_inc(s_t.h, 1)
            te(f)

            ot = otmp2.bitcast(F32)[:, buf * D:(buf + 1) * D]
            stp = s_stp[buf]
            st_prev = stp.n
            nep = s_v.inc()
            ep_done[t] = nep
            BK["p5_ep_prev"] = nep
            def f(v, t=t, xt=xt, ntm=ntm, ndma=ndma, ot=ot, stp=stp, st_prev=st_prev,
                  sx=sx):
                v.wait_ge(s_t.h, ntm)
                v.wait_ge(sx.h, ndma)
                if st_prev:
                    v.wait_ge(stp.h, st_prev)
                for (bi, c0, cw) in DSPL:
                    v.tensor_tensor(out=ot[:, c0:c0 + cw], in0=PS[bi][:, 0:cw],
                                    in1=g_out[:, t * D + c0:t * D + c0 + cw], op=OP.mult)
                v.tensor_tensor(out=ot, in0=ot, in1=xt, op=OP.add)
                v.drain().then_inc(s_v.h, 1)
            ve(f)

            nst = stp.inc(16)
            def f(e, t=t, nep=nep, ot=ot, stp=stp):
                e.wait_ge(s_v.h, nep)
                e.dma_start(out_d[t * 128:(t + 1) * 128, :], ot).then_inc(stp.h, 16)
            sy(f)

        nst0, nst1 = s_stp[0].n, s_stp[1].n
        nstf = s_st.n
        def f(e, nstf=nstf, nst0=nst0, nst1=nst1):
            if nstf:
                e.wait_ge(s_st.h, nstf)
            e.wait_ge(s_stp[0].h, nst0)
            e.wait_ge(s_stp[1].h, nst1)
        sy(f)

    nph = cfg.get("nphases", 6)
    with nc.Block() as b:
        blkref["b"] = b
        phases = [p0, p1, p2, p3, p4, p5]
        for ph in phases[:nph]:
            ph()

    nc.finalize()
    return nc


_CACHE = {}


def _perm_for(half, NT_R):
    own, other = [], []
    for i in range(NT_R):
        own.extend(range(256 * i + 128 * half, 256 * i + 128 * half + 128))
        other.extend(range(256 * i + 128 * (1 - half), 256 * i + 128 * (1 - half) + 128))
    return np.array(own + other, dtype=np.int64), np.array(own, dtype=np.int64)


def kernel(L0_attn_tokens, L0_attn_norm_w, L0_attn_Wq, L0_attn_Wkv, L0_attn_Wout,
           L0_attn_Wg, L0_ffn_Wq, L0_ffn_bq, L0_ffn_K1, L0_ffn_K2, L0_ffn_W_up,
           L0_ffn_W_down):
    cfg = full_cfg()
    NT_R = cfg["NT_R"]
    x = np.ascontiguousarray(np.asarray(L0_attn_tokens, dtype=np.float32))
    Bb, S, D = x.shape
    weights = dict(
        nw=np.ascontiguousarray(L0_attn_norm_w, dtype=np.float32),
        wq=np.ascontiguousarray(L0_attn_Wq, dtype=np.float32),
        wkv=np.ascontiguousarray(L0_attn_Wkv, dtype=np.float32),
        wout=np.ascontiguousarray(L0_attn_Wout, dtype=np.float32),
        wg=np.ascontiguousarray(L0_attn_Wg, dtype=np.float32),
        wf=np.ascontiguousarray(L0_ffn_Wq, dtype=np.float32),
        bq=np.ascontiguousarray(L0_ffn_bq, dtype=np.float32),
        k1=np.ascontiguousarray(L0_ffn_K1, dtype=np.float32),
        k2=np.ascontiguousarray(L0_ffn_K2, dtype=np.float32),
        wdn=np.ascontiguousarray(L0_ffn_W_down, dtype=np.float32),
        wup=np.ascontiguousarray(L0_ffn_W_up, dtype=np.float32),
    )
    if "nc" not in _CACHE:
        _CACHE["nc"] = build(cfg)
    nc = _CACHE["nc"]

    in_maps = []
    perms = []
    for c in range(8):
        b, half = c // 2, c % 2
        perm, own = _perm_for(half, NT_R)
        perms.append(own)
        m = dict(weights)
        m["xe"] = np.ascontiguousarray(x[b][perm])
        m["cflag"] = np.full((128, 1), float(half), dtype=np.float32)
        in_maps.append(m)

    res = run_bass_kernel_spmd(nc, in_maps, core_ids=list(range(8)))
    out = np.empty((Bb, S, D), dtype=np.float32)
    for c in range(8):
        b = c // 2
        out[b][perms[c]] = res.results[c]["out"]
    return out


# version 06 (final)
# speedup vs baseline: 1.4673x; 1.1261x over previous
import numpy as np
import concourse.bass as bass
import concourse.mybir as mybir
from concourse.bass import IndirectOffsetOnAxis
from concourse.bass_utils import run_bass_kernel_spmd

F32 = mybir.dt.float32
BF16 = mybir.dt.bfloat16
I32 = mybir.dt.int32
U32 = mybir.dt.uint32
U8 = mybir.dt.uint8
AF = mybir.ActivationFunctionType
OP = mybir.AluOpType
AX = mybir.AxisListType

NEG = -3.0e38
EXP_OFF = -4.0
LN_EPS = 1e-5
RMS_EPS = 1.1920929e-07


def full_cfg():
    return dict(D=1024, HATT=16, HFFN=8, NS=256, NEXP=65536, NT_R=16, taps=())


class Sem:
    def __init__(self, nc, name):
        self.h = nc.alloc_semaphore(name)
        self.n = 0

    def inc(self, k=1):
        self.n += k
        return self.n


def build(cfg):
    D = cfg["D"]; NCH = D // 128
    HATT = cfg["HATT"]; DH = 64
    HFFN = cfg["HFFN"]; NS = cfg["NS"]; NEXP = cfg["NEXP"]
    NT_R = cfg["NT_R"]; R = NT_R * 128
    NT_E = 2 * NT_R; E = 2 * R
    NQS = NT_R // 4
    DI = HATT * DH
    NDI = DI // 128
    KV = 2 * DI
    FQ = HFFN * 256
    NFC = FQ // 128
    NPAIR = HATT // 2
    NSH = NS // 128
    DSPL = [(i, i * 512, min(512, D - i * 512)) for i in range((D + 511) // 512)]
    taps = set(cfg.get("taps", ()))

    nc = bass.Bass("TRN2", target_bir_lowering=False, detect_race_conditions=False)

    xe = nc.dram_tensor("xe", [E, D], F32, kind="ExternalInput").ap()
    cflag = nc.dram_tensor("cflag", [128, 1], F32, kind="ExternalInput").ap()
    nw_in = nc.dram_tensor("nw", [D], F32, kind="ExternalInput").ap()
    wq_in = nc.dram_tensor("wq", [D, DI], F32, kind="ExternalInput").ap()
    wkv_in = nc.dram_tensor("wkv", [D, KV], F32, kind="ExternalInput").ap()
    wout_in = nc.dram_tensor("wout", [DI, D], F32, kind="ExternalInput").ap()
    wg_in = nc.dram_tensor("wg", [D, HATT], F32, kind="ExternalInput").ap()
    wf_in = nc.dram_tensor("wf", [D, FQ], F32, kind="ExternalInput").ap()
    bq_in = nc.dram_tensor("bq", [FQ], F32, kind="ExternalInput").ap()
    k1_in = nc.dram_tensor("k1", [NS, 128], F32, kind="ExternalInput").ap()
    k2_in = nc.dram_tensor("k2", [NS, 128], F32, kind="ExternalInput").ap()
    wdn_in = nc.dram_tensor("wdn", [NEXP, D], F32, kind="ExternalInput").ap()
    wup_in = nc.dram_tensor("wup", [NEXP, D], F32, kind="ExternalInput").ap()
    out_d = nc.dram_tensor("out", [R, D], F32, kind="ExternalOutput").ap()
    tap_d = {}
    def mktap(name, cols, dtype=F32):
        if name in taps:
            tap_d[name] = nc.dram_tensor("t_" + name, [128, cols], dtype,
                                         kind="ExternalOutput").ap()
    mktap("xn2fem", NCH * E, BF16); mktap("rinv", NT_E); mktap("ffnq", 2 * 512, BF16)
    mktap("s12", 2 * NS); mktap("ei", NT_R * HFFN * 10, I32)
    mktap("sc", NT_R * HFFN * 10); mktap("gff", NT_R * D, BF16)
    mktap("gates", R, BF16); mktap("gatedT", NPAIR * R, BF16)
    mktap("kpair", E, BF16)

    def dotap(name, src_ap, wait_sem, wait_val):
        if name not in tap_d:
            return
        tv = s_st.inc(16)
        def f(e, src_ap=src_ap, wait_sem=wait_sem, wait_val=wait_val, name=name):
            e.wait_ge(wait_sem.h, wait_val)
            e.dma_start(tap_d[name], src_ap).then_inc(s_st.h, 16)
        sy(f)
        s_v.inc()
        def f(v, tv=tv):
            v.wait_ge(s_st.h, tv)
            v.engine_nop()
            v.drain().then_inc(s_v.h, 1)
        ve(f)

    A = nc.alloc_sbuf_tensor
    xn2fem = A("xn2fem", [128, NCH * E], BF16).ap()
    g_out = A("g_out", [128, NT_R * D], BF16).ap()
    gatedT = A("gatedT", [128, NPAIR * R], BF16).ap()
    ei_all = A("ei_all", [128, NT_R * HFFN * 10], I32).ap()
    sc_all = A("sc_all", [128, NT_R * HFFN * 10], F32).ap()
    gstg = A("gstg", [128, R], BF16).ap()
    rep_c1 = A("rep_c1", [128, HFFN * 2 * NS], BF16).ap()
    wg_sb = A("wg_sb", [128, NCH * HATT], BF16).ap()
    k1t = A("k1t", [128, NS], BF16).ap()
    k2t = A("k2t", [128, NS], BF16).ap()
    ident = A("ident", [128, 128], BF16).ap()
    ones_bf = A("ones_bf", [128, 128], BF16).ap()
    onescol = A("onescol", [128, 1], BF16).ap()
    tril = A("tril", [128, 128], BF16).ap()
    iota10 = A("iota10", [128, 100], F32).ap()
    cfl = A("cfl", [128, 1], F32).ap()
    nw_f = A("nw_f", [128, NCH], F32).ap()
    bq_f = A("bq_f", [128, 2 * HFFN], F32).ap()
    bq_bf = A("bq_bf", [128, 2 * HFFN], BF16).ap()
    c12 = A("c12", [128, 2 * NS], F32).ap()
    rinv = A("rinv", [128, NT_E], F32).ap()
    stat = A("stat", [128, 16], F32).ap()
    cexp = A("cexp", [128, 1], F32).ap()
    wld = A("wld", [128, 2 * 512], F32).ap()
    topv = A("topv", [128, 32], F32).ap()
    topi = A("topi", [128, 32], U32).ap()
    comb = A("comb", [128, 160], F32).ap()
    crep = A("crep", [128, 160], F32).ap()
    cv8 = A("cv8", [128, 16], F32).ap()
    ci8 = A("ci8", [128, 16], U32).ap()
    bji = A("bji", [128, 16], U32).ap()
    bjf = A("bjf", [128, 32], F32).ap()
    i1f = A("i1f", [128, 32], F32).ap()
    mskt = A("mskt", [128, 200], F32).ap()
    e12 = A("e12", [128, 20], F32).ap()
    eff = A("eff", [128, 16], F32).ap()
    valst = A("valst", [128, 16], F32).ap()
    pval = A("pval", [128, 16], F32).ap()
    hcol = A("hcol", [128, 16], F32).ap()
    hgel = A("hgel", [128, 16], F32).ap()
    diag = A("diag", [128, 2 * 128], BF16).ap()
    dmpA = A("dmpA", [128, D], BF16).ap()

    AR = 40 * 1024
    arena = A("arena", [128, AR], U8).ap()

    def carve(off, n, dtype):
        sz = 4 if dtype in (F32, I32, U32) else 2
        assert off + n * sz <= AR, (off, n)
        return arena[:, off:off + n * sz].bitcast(dtype)

    wfp = carve(0, NCH * 256, BF16)
    ffnq = carve(NCH * 256 * 2, 2 * 512, BF16)
    _o2 = NCH * 256 * 2 + 2 * 512 * 2
    s12 = carve(_o2, 2 * NS, F32)
    srep = carve(_o2 + 2 * NS * 4, 2 * NS, F32)
    assert _o2 + 4 * NS * 4 <= 28 * 1024

    NGP = 5
    gbuf = carve(0, 2 * NGP * D, BF16)
    dmp3 = carve(2 * NGP * D * 2, D, BF16)
    assert 2 * NGP * D * 2 + D * 2 <= 28 * 1024
    xn_tok = carve(28 * 1024, 2 * D, BF16)

    _o = 0
    wkvq_a = carve(_o, NCH * 128, BF16); _o += NCH * 128 * 2
    kpair = carve(_o, E, BF16); _o += E * 2
    v2 = carve(_o, 2 * NT_E * 64, BF16); _o += 2 * NT_E * 64 * 2
    qpair = carve(_o, R, BF16); _o += R * 2
    NPT = 8
    pt = carve(_o, 3 * 512, BF16); _o += 3 * 512 * 2
    lrow = carve(_o, 512, F32); _o += 512 * 4
    glrow = carve(_o, 512, BF16)
    fin = carve(_o, 512, BF16); _o += 512 * 2
    assert _o <= 28 * 1024, _o
    glstg = carve(28 * 1024, 2 * 512, BF16)
    PTS = [pt[:, j * 512:(j + 1) * 512] for j in range(3)]
    PTS.append(carve(30 * 1024, 512, BF16))
    PTS += [carve(32 * 1024 + (j - 4) * 1024, 512, BF16) for j in range(4, 8)]

    wout_a = carve(0, NDI * D, BF16)
    otmp2 = carve(NDI * D * 2, 2 * D, F32)
    assert NDI * D * 2 + 2 * D * 4 <= 28 * 1024

    xload = carve(AR - 2 * D * 4, 2 * D, F32)
    assert 32 * 1024 <= AR - 2 * D * 4

    PS = [nc.alloc_psum_tensor(f"ps{i}", [128, 512], F32).ap() for i in range(8)]
    PSB = [p.bitcast(BF16) for p in PS]

    names = ["s_x", "s_w", "s_gp", "s_c", "s_v", "s_s", "s_t", "s_gd", "s_gc",
             "s_um", "s_dg", "s_pt", "s_av", "s_fz", "s_sg", "s_st"]
    SM = {n: Sem(nc, n) for n in names}
    (s_x, s_w, s_gp, s_c, s_v, s_s, s_t, s_gd, s_gc, s_um, s_dg, s_pt, s_av,
     s_fz, s_sg, s_st) = [SM[n] for n in names]
    s_xp = [Sem(nc, f"s_xp{i}") for i in range(2)]
    s_wp = [Sem(nc, f"s_wp{i}") for i in range(2)]
    s_sgp = [Sem(nc, f"s_sgp{i}") for i in range(2)]
    s_stp = [Sem(nc, f"s_stp{i}") for i in range(2)]
    s_gdn = [Sem(nc, f"s_gdn{i}") for i in range(NGP)]
    s_gup = [Sem(nc, f"s_gup{i}") for i in range(NGP)]

    BK = {}
    blkref = {}
    def sy(f): blkref["b"].sync(f)
    def gp(f): blkref["b"].gpsimd(f)
    def ve(f): blkref["b"].vector(f)
    def sa(f): blkref["b"].scalar(f)
    def te(f): blkref["b"].tensor(f)

    def _p0fin():
        BK["rep_c1_c"] = s_c.n
        BK["p0_c"] = s_c.n
        BK["p0_t"] = s_t.n

    def p0():
        def f(g):
            g.iota(iota10.rearrange("p (k j) -> p k j", j=10), [[0, 10], [1, 10]],
                   channel_multiplier=0, allow_small_or_imprecise_dtypes=True)
            g.iota(mskt[:, 0:128], [[1, 128]], channel_multiplier=-1,
                   allow_small_or_imprecise_dtypes=True).then_inc(s_gp.h, s_gp.inc())
        gp(f)

        nx = s_x.inc(48)
        def f(e):
            e.dma_start(cfl, cflag).then_inc(s_x.h, 16)
            e.dma_start(nw_f, nw_in.rearrange("(c p) -> p c", p=128),
                        allow_slow_non_contiguous=True).then_inc(s_x.h, 16)
            e.dma_start(bq_f, bq_in.rearrange("(hh d) -> d hh", d=128),
                        allow_slow_non_contiguous=True).then_inc(s_x.h, 16)
        sy(f)

        ngp, nxc = s_gp.n, nx
        ncn = s_c.inc()
        def f(v):
            v.wait_ge(s_gp.h, ngp)
            v.wait_ge(s_x.h, nxc)
            v.tensor_scalar(out=ident, in0=mskt[:, 0:128], scalar1=0.0, scalar2=None,
                            op0=OP.is_equal)
            v.tensor_scalar(out=tril, in0=mskt[:, 0:128], scalar1=0.0, scalar2=None,
                            op0=OP.is_ge)
            v.memset(ones_bf, 1.0)
            v.memset(onescol, 1.0)
            v.memset(cexp, EXP_OFF)
            v.memset(gstg, 0.0)
            v.memset(sc_all, 0.0)
            v.memset(hcol, 0.0)
            v.tensor_copy(bq_bf, bq_f)
            v.engine_nop()
            v.drain().then_inc(s_c.h, 1)
        ve(f)
        BK["consts"] = ncn
        if cfg.get('p0cut') == 1: return _p0fin()

        prev_cast = [ncn, ncn]
        piece = 0
        for ch in range(NCH):
            buf = piece % 2
            st_ap = wld[:, buf * 512:buf * 512 + HATT]
            need = prev_cast[buf]
            sw = s_wp[buf]
            ndma = sw.inc(16)
            def f(e, st_ap=st_ap, ch=ch, need=need, sw=sw):
                e.wait_ge(s_c.h, need)
                e.dma_start(st_ap, wg_in[ch * 128:(ch + 1) * 128, :]).then_inc(sw.h, 16)
            sy(f)
            ncst = s_c.inc()
            def f(v, st_ap=st_ap, ch=ch, ndma=ndma, sw=sw):
                v.wait_ge(sw.h, ndma)
                v.tensor_scalar(out=wg_sb[:, ch * HATT:(ch + 1) * HATT], in0=st_ap,
                                scalar1=nw_f[:, ch:ch + 1], scalar2=None, op0=OP.mult)
                v.drain().then_inc(s_c.h, 1)
            ve(f)
            prev_cast[buf] = ncst
            piece += 1
        if cfg.get('p0cut') == 2: return _p0fin()

        for (k_in, k_t) in ((k1_in, k1t), (k2_in, k2t)):
            for half in range(NSH):
                buf = piece % 2
                st_ap = wld[:, buf * 512:buf * 512 + 128]
                need = prev_cast[buf]
                sw = s_wp[buf]
                ndma = sw.inc(16)
                def f(e, st_ap=st_ap, k_in=k_in, half=half, need=need, sw=sw):
                    e.wait_ge(s_c.h, need)
                    e.dma_start(st_ap, k_in[half * 128:half * 128 + 128, :]) \
                     .then_inc(sw.h, 16)
                sy(f)
                ncst = s_c.inc()
                def f(v, st_ap=st_ap, half=half, ndma=ndma, sw=sw):
                    v.wait_ge(sw.h, ndma)
                    v.tensor_copy(dmpA[:, half * 128:half * 128 + 128], st_ap)
                    v.drain().then_inc(s_c.h, 1)
                ve(f)
                prev_cast[buf] = ncst
                piece += 1
            ncnow = s_c.n
            ntp = s_t.inc()
            def f(t, ncnow=ncnow):
                t.wait_ge(s_c.h, ncnow)
                mm = None
                for half in range(NSH):
                    mm = t.transpose(PSB[0][:, half * 128:half * 128 + 128],
                                     dmpA[:, half * 128:half * 128 + 128], ident)
                mm.then_inc(s_t.h, 1)
            te(f)
            ncst = s_c.inc()
            def f(v, k_t=k_t, ntp=ntp):
                v.wait_ge(s_t.h, ntp)
                v.tensor_copy(k_t, PSB[0][:, 0:NS])
                v.drain().then_inc(s_c.h, 1)
            ve(f)
            prev_cast[0] = prev_cast[1] = ncst
        if cfg.get('p0cut') == 3: return _p0fin()

        c12bf = c12.bitcast(BF16)
        for h in range(HFFN):
            ncnow = s_c.n
            ntp = s_t.inc()
            def f(t, h=h, ncnow=ncnow):
                t.wait_ge(s_c.h, ncnow)
                t.matmul(PS[0][0:1, 0:NS], bq_bf[:, 2 * h:2 * h + 1], k1t,
                         start=True, stop=True)
                t.matmul(PS[0][0:1, NS:2 * NS], bq_bf[:, 2 * h + 1:2 * h + 2], k2t,
                         start=True, stop=True).then_inc(s_t.h, 1)
            te(f)
            ncev = s_c.inc()
            def f(v, ntp=ntp):
                v.wait_ge(s_t.h, ntp)
                v.tensor_copy(c12bf[0:1, 0:2 * NS], PS[0][0:1, 0:2 * NS])
                v.drain().then_inc(s_c.h, 1)
            ve(f)
            ntp2 = s_t.inc()
            def f(t, ncev=ncev):
                t.wait_ge(s_c.h, ncev)
                t.matmul(PS[1][:, 0:2 * NS], ones_bf[0:1, 0:128],
                         c12bf[0:1, 0:2 * NS], start=True, stop=True) \
                 .then_inc(s_t.h, 1)
            te(f)
            ncst = s_c.inc()
            def f(v, h=h, ntp2=ntp2):
                v.wait_ge(s_t.h, ntp2)
                v.tensor_copy(rep_c1[:, h * 2 * NS:(h + 1) * 2 * NS],
                              PS[1][:, 0:2 * NS])
                v.drain().then_inc(s_c.h, 1)
            ve(f)
        BK["rep_c1_c"] = s_c.n
        BK["p0_c"] = s_c.n
        BK["p0_t"] = s_t.n

    def p1():
        xn2_done = {}
        prev_evac = 0
        p0c, p0t = BK["p0_c"], BK["p0_t"]
        for t in range(NT_E):
            buf = t % 2
            xt = xload[:, buf * D:(buf + 1) * D]
            x2t = xn_tok[:, buf * D:(buf + 1) * D]
            need = xn2_done.get(t - 2, 0)
            sx = s_xp[buf]
            ndma = sx.inc(16)
            def f(e, xt=xt, t=t, need=need, sx=sx):
                if need:
                    e.wait_ge(s_s.h, need)
                e.dma_start(xt, xe[t * 128:(t + 1) * 128, :]).then_inc(sx.h, 16)
            sy(f)

            nst = s_v.inc()
            def f(v, xt=xt, ndma=ndma, t=t, p0t=p0t, sx=sx):
                v.wait_ge(sx.h, ndma)
                if t == 0:
                    v.wait_ge(s_t.h, p0t)
                v.tensor_scalar(out=dmpA, in0=xt, scalar1=1.0 / D, scalar2=0.0,
                                op0=OP.mult, op1=OP.add, accum_out=stat[:, 0:1])
                v.scalar_tensor_tensor(out=dmpA, in0=xt, scalar=1.0 / D, in1=xt,
                                       op0=OP.mult, op1=OP.mult, accum_out=stat[:, 1:2])
                v.drain()
                v.tensor_tensor(out=stat[:, 2:3], in0=stat[:, 0:1], in1=stat[:, 0:1],
                                op=OP.mult)
                v.drain()
                v.tensor_sub(stat[:, 3:4], stat[:, 1:2], stat[:, 2:3])
                v.drain()
                v.tensor_scalar(out=stat[:, 4:5], in0=stat[:, 3:4], scalar1=LN_EPS,
                                scalar2=None, op0=OP.add)
                v.tensor_scalar(out=stat[:, 5:6], in0=stat[:, 3:4],
                                scalar1=1.0 + RMS_EPS, scalar2=LN_EPS * RMS_EPS,
                                op0=OP.mult, op1=OP.add)
                v.drain().then_inc(s_v.h, 1)
            ve(f)

            nsq = s_s.inc()
            def f(s, nst=nst):
                s.wait_ge(s_v.h, nst)
                s.activation(out=stat[:, 6:8], in_=stat[:, 4:6], func=AF.Sqrt)
                s.drain().then_inc(s_s.h, 1)
            sa(f)

            nsb = s_v.inc()
            def f(v, nsq=nsq, t=t):
                v.wait_ge(s_s.h, nsq)
                v.reciprocal(stat[:, 8:10], stat[:, 6:8])
                v.drain()
                v.tensor_tensor(out=rinv[:, t:t + 1], in0=stat[:, 8:9], in1=stat[:, 7:8],
                                op=OP.mult)
                v.drain()
                v.tensor_scalar(out=stat[:, 10:11], in0=stat[:, 0:1],
                                scalar1=stat[:, 9:10], scalar2=-1.0,
                                op0=OP.mult, op1=OP.mult)
                v.drain().then_inc(s_v.h, 1)
            ve(f)

            nxn = s_s.inc()
            xn2_done[t] = nxn
            def f(s, xt=xt, x2t=x2t, nsb=nsb):
                s.wait_ge(s_v.h, nsb)
                s.activation(out=x2t, in_=xt, func=AF.Identity, bias=stat[:, 10:11],
                             scale=stat[:, 9:10])
                s.drain().then_inc(s_s.h, 1)
            sa(f)

            ntp = s_t.inc()
            def f(t_, x2t=x2t, nxn=nxn, need_ps=prev_evac, t=t, p0c=p0c):
                t_.wait_ge(s_s.h, nxn)
                if t == 0:
                    t_.wait_ge(s_c.h, p0c)
                if need_ps:
                    t_.wait_ge(s_v.h, need_ps)
                mm = None
                for ch in range(NCH):
                    mm = t_.transpose(PSB[0][:, ch * 128:(ch + 1) * 128],
                                      x2t[:, ch * 128:(ch + 1) * 128], ident)
                mm.then_inc(s_t.h, 1)
            te(f)

            nev = s_v.inc()
            def f(v, t=t, ntp=ntp):
                v.wait_ge(s_t.h, ntp)
                for ch in range(NCH):
                    v.tensor_copy(xn2fem[:, ch * E + t * 128:ch * E + (t + 1) * 128],
                                  PSB[0][:, ch * 128:(ch + 1) * 128])
                v.engine_nop()
                v.drain().then_inc(s_v.h, 1)
            ve(f)
            prev_evac = nev
        BK["p1_v"] = s_v.n
        BK["p1_s"] = s_s.n
        dotap("xn2fem", xn2fem, s_v, s_v.n)
        dotap("rinv", rinv[:, 0:NT_E], s_v, s_v.n)

    def p2():
        p1v = BK["p1_v"]
        gate_ev = []
        for qb in range(R // 512):
            bank = qb % 2
            ntm = s_t.inc()
            guard = gate_ev[-2] if len(gate_ev) >= 2 else 0
            def f(t, qb=qb, bank=bank, p1v=p1v, guard=guard):
                if qb == 0:
                    t.wait_ge(s_v.h, p1v)
                    t.wait_ge(s_c.h, BK["p0_c"])
                if guard:
                    t.wait_ge(s_s.h, guard)
                mm = None
                for ch in range(NCH):
                    mm = t.matmul(PS[bank][0:HATT, 0:512],
                                  wg_sb[:, ch * HATT:(ch + 1) * HATT],
                                  xn2fem[:, ch * E + qb * 512:ch * E + (qb + 1) * 512],
                                  start=(ch == 0), stop=(ch == NCH - 1))
                mm.then_inc(s_t.h, 1)
            te(f)
            nsg = s_s.inc()
            gate_ev.append(nsg)
            def f(s, qb=qb, bank=bank, ntm=ntm):
                s.wait_ge(s_t.h, ntm)
                s.activation(out=gstg[0:HATT, qb * 512:(qb + 1) * 512],
                             in_=PS[bank][0:HATT, 0:512], func=AF.Sigmoid)
                s.drain().then_inc(s_s.h, 1)
            sa(f)
        BK["gates_s"] = s_s.n
        dotap("gates", gstg[:, 0:R], s_s, s_s.n)

        stream_block.piece = 0
        stream_block.prev = [BK["p0_c"], BK["p0_c"]]
        gsn = BK["gates_s"]
        for h in range(HFFN):
            wfr = stream_block(wf_in, wfp, 0, NCH, h * 256, 256,
                               extra_s=BK.get("p2_ffq_last", 0))
            for gi in range(NQS):
                ntm = s_t.inc()
                bank_guard = BK.get("p2_ffq_last", 0)
                def f(t, h=h, gi=gi, wfr=wfr, gsn=gsn, bank_guard=bank_guard):
                    if gi == 0:
                        t.wait_ge(s_c.h, wfr)
                    t.wait_ge(s_s.h, max(gsn, bank_guard))
                    mm = None
                    for half in range(2):
                        for ch in range(NCH):
                            mm = t.matmul(PS[half][:, 0:512],
                                     wfp[:, ch * 256 + half * 128:ch * 256 + half * 128 + 128],
                                     xn2fem[:, ch * E + gi * 512:ch * E + (gi + 1) * 512],
                                     start=(ch == 0), stop=(ch == NCH - 1))
                    mm.then_inc(s_t.h, 1)
                te(f)
                nev = s_s.inc(2)
                def f(s, ntm=ntm):
                    s.wait_ge(s_t.h, ntm)
                    s.copy(ffnq[:, 0:512], PS[0][:, 0:512])
                    s.drain().then_inc(s_s.h, 1)
                    s.copy(ffnq[:, 512:1024], PS[1][:, 0:512])
                    s.drain().then_inc(s_s.h, 1)
                sa(f)
                BK["p2_ffq_last"] = nev
                if gi == 0 and h == 0:
                    dotap("ffnq", ffnq, s_s, s_s.n)
                    BK["p2_ffq_last"] = max(BK["p2_ffq_last"], 0)

                for tk_ in range(4):
                    t_glob = gi * 4 + tk_
                    p2_tile_head(tk_, t_glob, h)
        BK["p2_v"] = s_v.n
        dotap("ei", ei_all, s_v, s_v.n)
        dotap("sc", sc_all, s_v, s_v.n)

    def p2_tile_head(tk_, t_glob, h):
        ffq_evac = BK["p2_ffq_last"]
        fix_prev = BK.get("p2_fix_prev", 0)
        ntm = s_t.inc()
        def f(t, h=h, tk_=tk_, ffq_evac=ffq_evac, fix_prev=fix_prev):
            t.wait_ge(s_s.h, ffq_evac)
            if fix_prev:
                t.wait_ge(s_v.h, fix_prev)
            t.matmul(PS[2][:, 0:NS],
                     ffnq[:, tk_ * 128:(tk_ + 1) * 128],
                     k1t, start=True, stop=True)
            t.matmul(PS[2][:, NS:2 * NS],
                     ffnq[:, 512 + tk_ * 128:512 + (tk_ + 1) * 128],
                     k2t, start=True, stop=True).then_inc(s_t.h, 1)
        te(f)

        nfix = s_v.inc()
        BK["p2_fix_prev"] = nfix
        def f(v, t_glob=t_glob, h=h, ntm=ntm):
            v.wait_ge(s_t.h, ntm)
            v.wait_ge(s_c.h, BK["rep_c1_c"])
            v.scalar_tensor_tensor(out=s12, in0=PS[2][:, 0:2 * NS],
                                   scalar=rinv[:, t_glob:t_glob + 1],
                                   in1=rep_c1[:, h * 2 * NS:(h + 1) * 2 * NS],
                                   op0=OP.mult, op1=OP.add)
            v.drain().then_inc(s_v.h, 1)
        ve(f)
        if t_glob == 0 and h == 0:
            dotap("s12", s12, s_v, s_v.n)

        ntop = s_v.inc()
        def f(v, t_glob=t_glob, h=h):
            src0 = s12[:, 0:NS]
            src1 = s12[:, NS:2 * NS]
            rep0 = srep[:, 0:NS]
            rep1 = srep[:, NS:2 * NS]
            vv0 = topv[:, 0:16]
            vv1 = topv[:, 16:32]
            ii0 = topi[:, 0:16]
            ii1 = topi[:, 16:32]
            v.max(out=vv0[:, 0:8], in_=src0)
            v.max(out=vv1[:, 0:8], in_=src1)
            v.drain()
            v.max_index(out=ii0[:, 0:8], in_max=vv0[:, 0:8], in_values=src0)
            v.max_index(out=ii1[:, 0:8], in_max=vv1[:, 0:8], in_values=src1)
            v.drain()
            v.match_replace(out=rep0, in_to_replace=vv0[:, 0:8], in_values=src0,
                            imm_value=NEG)
            v.match_replace(out=rep1, in_to_replace=vv1[:, 0:8], in_values=src1,
                            imm_value=NEG)
            v.drain()
            v.max(out=vv0[:, 8:16], in_=rep0)
            v.max(out=vv1[:, 8:16], in_=rep1)
            v.drain()
            v.max_index(out=ii0[:, 8:16], in_max=vv0[:, 8:16], in_values=rep0)
            v.max_index(out=ii1[:, 8:16], in_max=vv1[:, 8:16], in_values=rep1)
            v.memset(comb, NEG)
            v.drain()
            v.tensor_tensor(
                out=comb.rearrange("p (j l) -> p j l", l=16)[:, 0:10, 0:10],
                in0=topv[:, 0:10].unsqueeze(-1).broadcast_to([128, 10, 10]),
                in1=topv[:, 16:26].unsqueeze(1).broadcast_to([128, 10, 10]),
                op=OP.add)
            v.drain()
            v.max(out=cv8[:, 0:8], in_=comb)
            v.drain()
            v.match_replace(out=crep, in_to_replace=cv8[:, 0:8], in_values=comb,
                            imm_value=NEG)
            v.max_index(out=ci8[:, 0:8], in_max=cv8[:, 0:8], in_values=comb)
            v.drain()
            v.max(out=cv8[:, 8:16], in_=crep)
            v.drain()
            v.max_index(out=ci8[:, 8:16], in_max=cv8[:, 8:16], in_values=crep)
            v.drain()
            v.tensor_copy(bji[:, 0:8], ci8[:, 0:8])
            v.tensor_copy(bji[:, 8:10], ci8[:, 8:10])
            v.tensor_copy(i1f[:, 0:10], topi[:, 0:10])
            v.tensor_copy(i1f[:, 16:26], topi[:, 16:26])
            v.reduce_max(stat[:, 14:15], cv8[:, 0:10], axis=AX.X)
            v.drain()
            v.tensor_scalar(out=ci8[:, 0:10], in0=bji[:, 0:10], scalar1=4,
                            scalar2=None, op0=OP.logical_shift_right)
            v.tensor_scalar(out=valst.bitcast(U32)[:, 0:10], in0=bji[:, 0:10],
                            scalar1=15, scalar2=None, op0=OP.bitwise_and)
            v.tensor_scalar(out=stat[:, 15:16], in0=stat[:, 14:15], scalar1=-1.0,
                            scalar2=None, op0=OP.mult)
            v.drain()
            v.tensor_copy(bjf[:, 0:10], ci8[:, 0:10])
            v.tensor_copy(bjf[:, 10:20], valst.bitcast(U32)[:, 0:10])
            v.drain()
            mmA = mskt[:, 0:100].rearrange("p (k j) -> p k j", j=10)
            mmB = mskt[:, 100:200].rearrange("p (k j) -> p k j", j=10)
            v.tensor_tensor(
                out=mmA,
                in0=bjf[:, 0:10].unsqueeze(-1).broadcast_to([128, 10, 10]),
                in1=iota10.rearrange("p (k j) -> p k j", j=10), op=OP.is_equal)
            v.tensor_tensor(
                out=mmB,
                in0=bjf[:, 10:20].unsqueeze(-1).broadcast_to([128, 10, 10]),
                in1=iota10.rearrange("p (k j) -> p k j", j=10), op=OP.is_equal)
            v.drain()
            v.tensor_tensor(
                out=mmA, in0=mmA,
                in1=i1f[:, 0:10].unsqueeze(1).broadcast_to([128, 10, 10]),
                op=OP.mult)
            v.tensor_tensor(
                out=mmB, in0=mmB,
                in1=i1f[:, 16:26].unsqueeze(1).broadcast_to([128, 10, 10]),
                op=OP.mult)
            v.drain()
            v.tensor_reduce(out=e12[:, 0:10], in_=mmA, axis=AX.X, op=OP.add)
            v.tensor_reduce(out=e12[:, 10:20], in_=mmB, axis=AX.X, op=OP.add)
            v.drain()
            v.scalar_tensor_tensor(out=eff[:, 0:10], in0=e12[:, 0:10],
                                   scalar=float(NS), in1=e12[:, 10:20],
                                   op0=OP.mult, op1=OP.add)
            v.drain()
            v.tensor_copy(ei_all[:, (t_glob * HFFN + h) * 10:(t_glob * HFFN + h) * 10 + 10],
                          eff[:, 0:10])
            v.drain().then_inc(s_v.h, 1)
        ve(f)

        nse = s_s.inc()
        def f(s, ntop=ntop):
            s.wait_ge(s_v.h, ntop)
            s.activation(out=pval[:, 0:10], in_=cv8[:, 0:10], func=AF.Exp,
                         bias=stat[:, 15:16], accum_out=stat[:, 13:14])
            s.drain().then_inc(s_s.h, 1)
        sa(f)
        nsc = s_v.inc()
        def f(v, t_glob=t_glob, h=h, nse=nse):
            v.wait_ge(s_s.h, nse)
            v.reciprocal(stat[:, 12:13], stat[:, 13:14])
            v.drain()
            v.tensor_scalar(out=sc_all[:, (t_glob * HFFN + h) * 10:(t_glob * HFFN + h) * 10 + 10],
                            in0=pval[:, 0:10], scalar1=stat[:, 12:13], scalar2=None,
                            op0=OP.mult)
            v.drain().then_inc(s_v.h, 1)
        ve(f)

    def p3():
        unit = 0
        um_done = {}
        p2v = BK["p2_v"]
        for t in range(NT_R):
            buf = t % 2
            xt = xn_tok[:, buf * D:(buf + 1) * D]
            prev_ev = BK.get("p3_xnev", 0)
            ntp = s_t.inc()
            def f(t_, t=t, prev_ev=prev_ev, p2v=p2v):
                if t == 0:
                    t_.wait_ge(s_v.h, p2v)
                if prev_ev:
                    t_.wait_ge(s_v.h, prev_ev)
                mm = None
                for ch in range(NCH):
                    mm = t_.transpose(PSB[1][:, ch * 128:(ch + 1) * 128],
                                      xn2fem[:, ch * E + t * 128:ch * E + (t + 1) * 128],
                                      ident)
                mm.then_inc(s_t.h, 1)
            te(f)
            nxe = s_v.inc()
            BK["p3_xnev"] = nxe
            def f(v, xt=xt, t=t, ntp=ntp):
                v.wait_ge(s_t.h, ntp)
                for ch in range(NCH):
                    v.tensor_scalar(out=xt[:, ch * 128:(ch + 1) * 128],
                                    in0=PSB[1][:, ch * 128:(ch + 1) * 128],
                                    scalar1=rinv[:, t:t + 1], scalar2=None, op0=OP.mult)
                v.engine_nop()
                v.drain().then_inc(s_v.h, 1)
            ve(f)
            if t == 0 and "hcol" in tap_d:
                pass

            silu_prev = BK.get("p3_silu", 0)
            for h in range(HFFN):
                for kp in range(5):
                    u0 = unit
                    pu = []
                    for dk in range(2):
                        u = u0 + dk
                        k = 2 * kp + dk
                        sl = u % NGP
                        gd_t = gbuf[:, (2 * sl) * D:(2 * sl + 1) * D]
                        gu_t = gbuf[:, (2 * sl + 1) * D:(2 * sl + 2) * D]
                        need_gc = max(0, u - NGP + 1)
                        need_um = um_done.get(u - NGP, 0)
                        ngdn = s_gdn[sl].inc(16)
                        ngup = s_gup[sl].inc(16)
                        idx_col = ei_all[:, (t * HFFN + h) * 10 + k:(t * HFFN + h) * 10 + k + 1]
                        def f(g, gd_t=gd_t, gu_t=gu_t, idx_col=idx_col, need_gc=need_gc,
                              need_um=need_um, p2v=p2v, u=u, sl=sl):
                            if u == 0:
                                g.wait_ge(s_v.h, p2v)
                            if need_gc:
                                g.wait_ge(s_gc.h, need_gc)
                            if need_um:
                                g.wait_ge(s_um.h, need_um)
                            g.indirect_dma_start(out=gd_t, out_offset=None, in_=wdn_in,
                                                 in_offset=IndirectOffsetOnAxis(ap=idx_col, axis=0)) \
                             .then_inc(s_gdn[sl].h, 16)
                            g.indirect_dma_start(out=gu_t, out_offset=None, in_=wup_in,
                                                 in_offset=IndirectOffsetOnAxis(ap=idx_col, axis=0)) \
                             .then_inc(s_gup[sl].h, 16)
                        gp(f)
                        pu.append((u, sl, ngdn, ngup, gd_t, gu_t))

                    kk0 = u0 % 16
                    ngc = s_gc.inc(2)
                    def f(v, pu=pu, xt=xt):
                        for (u_, sl_, ngdn_, ngup_, gd_, gu_) in pu:
                            v.wait_ge(s_gdn[sl_].h, ngdn_)
                            v.scalar_tensor_tensor(out=dmp3, in0=gd_, scalar=1.0, in1=xt,
                                                   op0=OP.mult, op1=OP.mult,
                                                   accum_out=hcol[:, (u_ % 16):(u_ % 16) + 1])
                        v.drain().then_inc(s_gc.h, 2)
                    ve(f)

                    nge = s_s.inc()
                    def f(s, kk0=kk0, ngc=ngc):
                        s.wait_ge(s_gc.h, ngc)
                        s.activation(out=hgel[:, kk0:kk0 + 2], in_=hcol[:, kk0:kk0 + 2],
                                     func=AF.Gelu)
                        s.drain().then_inc(s_s.h, 1)
                    sa(f)

                    need_um2 = um_done.get(u0 - 1, 0)
                    ndg = s_dg.inc()
                    def f(v, t=t, h=h, kp=kp, u0=u0, nge=nge, need_um2=need_um2):
                        v.wait_ge(s_s.h, nge)
                        if need_um2:
                            v.wait_ge(s_um.h, need_um2)
                        for dk in range(2):
                            u = u0 + dk
                            k = 2 * kp + dk
                            v.tensor_scalar(out=diag[:, (u % 2) * 128:(u % 2) * 128 + 128],
                                            in0=ident, scalar1=hgel[:, (u % 16):(u % 16) + 1],
                                            scalar2=sc_all[:, (t * HFFN + h) * 10 + k:
                                                           (t * HFFN + h) * 10 + k + 1],
                                            op0=OP.mult, op1=OP.mult)
                        v.drain().then_inc(s_dg.h, 1)
                    ve(f)

                    for dk in range(2):
                        u = u0 + dk
                        k = 2 * kp + dk
                        hk = h * 10 + k
                        num = s_um.inc()
                        um_done[u] = num
                        ngup = pu[dk][3]
                        def f(t_, u=u, hk=hk, ndg=ndg, silu_prev=silu_prev, ngup=ngup):
                            t_.wait_ge(s_dg.h, ndg)
                            sl = u % NGP
                            t_.wait_ge(s_gup[sl].h, ngup)
                            if hk == 0 and silu_prev:
                                t_.wait_ge(s_s.h, silu_prev)
                            gu_t = gbuf[:, (2 * sl + 1) * D:(2 * sl + 2) * D]
                            dg_t = diag[:, (u % 2) * 128:(u % 2) * 128 + 128]
                            mm = None
                            for (bi, c0, cw) in DSPL:
                                mm = t_.matmul(PS[3 + bi][:, 0:cw], dg_t, gu_t[:, c0:c0 + cw],
                                          start=(hk == 0), stop=(hk == HFFN * 10 - 1))
                            mm.then_inc(s_um.h, 1)
                        te(f)
                        unit += 1

            nsl = s_s.inc(len(DSPL))
            BK["p3_silu"] = nsl
            ulast = s_um.n
            def f(s, t=t, ulast=ulast):
                s.wait_ge(s_um.h, ulast)
                for (bi, c0, cw) in DSPL:
                    s.activation(out=g_out[:, t * D + c0:t * D + c0 + cw],
                                 in_=PS[3 + bi][:, 0:cw], func=AF.Silu)
                s.drain().then_inc(s_s.h, len(DSPL))
            sa(f)
        BK["p3_s"] = s_s.n
        dotap("gff", g_out, s_s, s_s.n)

    def stream_block(w_in, dst, rows0, nch, c0, cw, extra_s=0, extra_v=0,
                     fold=False, scale=1.0):
        for ch in range(nch):
            buf = stream_block.piece % 2
            st_ap = wld[:, buf * 512:buf * 512 + cw]
            need = stream_block.prev[buf]
            sw = s_wp[buf]
            ndma = sw.inc(16)
            def f(e, st_ap=st_ap, ch=ch, need=need, w_in=w_in, rows0=rows0, c0=c0,
                  cw=cw, extra_s=extra_s, extra_v=extra_v, sw=sw):
                e.wait_ge(s_c.h, need)
                if extra_s and ch == 0:
                    e.wait_ge(s_s.h, extra_s)
                if extra_v and ch == 0:
                    e.wait_ge(s_v.h, extra_v)
                e.dma_start(st_ap, w_in[rows0 + ch * 128:rows0 + ch * 128 + 128,
                                        c0:c0 + cw]).then_inc(sw.h, 16)
            sy(f)
            ncst = s_c.inc()
            def f(v, st_ap=st_ap, ch=ch, ndma=ndma, dst=dst, cw=cw, fold=fold,
                  scale=scale, sw=sw):
                v.wait_ge(sw.h, ndma)
                if fold:
                    v.tensor_scalar(out=dst[:, ch * cw:(ch + 1) * cw], in0=st_ap,
                                    scalar1=nw_f[:, ch:ch + 1], scalar2=float(scale),
                                    op0=OP.mult, op1=OP.mult)
                    v.drain().then_inc(s_c.h, 1)
                else:
                    v.tensor_copy(dst[:, ch * cw:(ch + 1) * cw], st_ap)
                    v.drain().then_inc(s_c.h, 1)
            ve(f)
            stream_block.prev[buf] = ncst
            stream_block.piece += 1
        return s_c.n

    def p4():
        stream_block.piece = 0
        stream_block.prev = [BK["p0_c"], BK["p0_c"]]
        p1v = BK["p1_v"]
        p3s = BK["p3_s"]
        BK["p4_glstg_hist"] = []
        BK["p4_unit"] = 0

        for hp in range(NPAIR):
            h0 = 2 * hp
            wkr = stream_block(wkv_in, wkvq_a, 0, NCH, h0 * DH, 128,
                               extra_s=(p3s if hp == 0 else 0),
                               extra_v=BK.get("p4_vev", 0), fold=True)
            prev_kev = BK.get("p4_kev", 0)
            ssn = s_s.n
            kev_hist = []
            for nb in range(E // 512):
                bank = 5 + nb % 2
                ntm = s_t.inc()
                guard = kev_hist[-2] if len(kev_hist) >= 2 else 0
                def f(t, nb=nb, bank=bank, wkr=wkr, prev_kev=prev_kev, p1v=p1v,
                      ssn=ssn, guard=guard):
                    if nb == 0:
                        t.wait_ge(s_c.h, wkr)
                        t.wait_ge(s_v.h, max(p1v, prev_kev))
                        t.wait_ge(s_s.h, ssn)
                    if guard:
                        t.wait_ge(s_v.h, guard)
                    mm = None
                    for ch in range(NCH):
                        mm = t.matmul(PS[bank][:, 0:512], wkvq_a[:, ch * 128:(ch + 1) * 128],
                                 xn2fem[:, ch * E + nb * 512:ch * E + (nb + 1) * 512],
                                 start=(ch == 0), stop=(ch == NCH - 1))
                    mm.then_inc(s_t.h, 1)
                te(f)
                nev = s_v.inc()
                kev_hist.append(nev)
                def f(v, nb=nb, bank=bank, ntm=ntm):
                    v.wait_ge(s_t.h, ntm)
                    v.tensor_copy(kpair[:, nb * 512:(nb + 1) * 512], PS[bank][:, 0:512])
                    v.drain().then_inc(s_v.h, 1)
                ve(f)
            if hp == 0:
                dotap("kpair", kpair, s_v, s_v.n)
            BK["p4_kev"] = s_v.n

            wqr = stream_block(wq_in, wkvq_a, 0, NCH, h0 * DH, 128,
                               extra_v=BK["p4_kev"], fold=True, scale=0.125)
            prev_qev = BK.get("p4_qev", 0)
            ssn = s_s.n
            qev_hist = []
            for nb in range(R // 512):
                bank = 5 + nb % 2
                ntm = s_t.inc()
                guard = qev_hist[-2] if len(qev_hist) >= 2 else 0
                def f(t, nb=nb, bank=bank, wqr=wqr, prev_qev=prev_qev, ssn=ssn,
                      guard=guard):
                    if nb == 0:
                        t.wait_ge(s_c.h, wqr)
                        if prev_qev:
                            t.wait_ge(s_v.h, prev_qev)
                        t.wait_ge(s_s.h, ssn)
                    if guard:
                        t.wait_ge(s_v.h, guard)
                    mm = None
                    for ch in range(NCH):
                        mm = t.matmul(PS[bank][:, 0:512], wkvq_a[:, ch * 128:(ch + 1) * 128],
                                 xn2fem[:, ch * E + nb * 512:ch * E + (nb + 1) * 512],
                                 start=(ch == 0), stop=(ch == NCH - 1))
                    mm.then_inc(s_t.h, 1)
                te(f)
                nev = s_v.inc()
                qev_hist.append(nev)
                def f(v, nb=nb, bank=bank, ntm=ntm):
                    v.wait_ge(s_t.h, ntm)
                    v.tensor_copy(qpair[:, nb * 512:(nb + 1) * 512], PS[bank][:, 0:512])
                    v.drain().then_inc(s_v.h, 1)
                ve(f)
            BK["p4_qev"] = s_v.n

            wvr = stream_block(wkv_in, wkvq_a, 0, NCH, DI + h0 * DH, 128,
                               extra_v=BK["p4_qev"], fold=True)
            prev_vev = BK.get("p4_vev", 0)
            ssn = s_s.n
            vev_hist = []
            for tt in range(NT_E):
                bank = 5 + tt % 2
                ntm = s_t.inc()
                guard = vev_hist[-2] if len(vev_hist) >= 2 else 0
                def f(t, tt=tt, bank=bank, wvr=wvr, prev_vev=prev_vev, ssn=ssn,
                      guard=guard):
                    if tt == 0:
                        t.wait_ge(s_c.h, wvr)
                        if prev_vev:
                            t.wait_ge(s_v.h, prev_vev)
                        t.wait_ge(s_s.h, ssn)
                    if guard:
                        t.wait_ge(s_v.h, guard)
                    mm = None
                    for ch in range(NCH):
                        mm = t.matmul(PS[bank][:, 0:128],
                                 xn2fem[:, ch * E + tt * 128:ch * E + (tt + 1) * 128],
                                 wkvq_a[:, ch * 128:(ch + 1) * 128],
                                 start=(ch == 0), stop=(ch == NCH - 1))
                    mm.then_inc(s_t.h, 1)
                te(f)
                nev = s_v.inc()
                vev_hist.append(nev)
                def f(v, tt=tt, bank=bank, ntm=ntm):
                    v.wait_ge(s_t.h, ntm)
                    for hh in range(2):
                        v.tensor_copy(v2[:, (hh * NT_E + tt) * 64:(hh * NT_E + tt) * 64 + 64],
                                      PS[bank][:, hh * 64:(hh + 1) * 64])
                    v.engine_nop()
                    v.drain().then_inc(s_v.h, 1)
                ve(f)
            BK["p4_vev"] = s_v.n

            for hh in range(2):
                h = 2 * hp + hh
                pe = 64 * hh
                for qs in range(NQS):
                    p4_head_qs(hp, hh, h, pe, qs)
        BK["p4_fz"] = s_fz.n
        dotap("gatedT", gatedT, s_fz, s_fz.n)

    def p4_head_qs(hp, hh, h, pe, qs):
        gqi = len(BK["p4_glstg_hist"])
        buf = gqi % 2
        sgp = s_sgp[buf]
        gmv = sgp.inc(16)
        gsr = max(BK["gates_s"], BK["p3_s"] if gqi == 0 else 0)
        guard = BK["p4_glstg_hist"][-2] if len(BK["p4_glstg_hist"]) >= 2 else 0
        def f(e, h=h, pe=pe, qs=qs, gsr=gsr, buf=buf, guard=guard, sgp=sgp):
            e.wait_ge(s_s.h, gsr)
            if guard:
                e.wait_ge(s_fz.h, guard)
            e.dma_start(glstg[pe:pe + 1, buf * 512:(buf + 1) * 512],
                        gstg[h:h + 1, qs * 512:(qs + 1) * 512]).then_inc(sgp.h, 16)
        sy(f)

        units = []
        for j in range(4 * qs):
            units.append((j, None))
        for a in range(4):
            units.append((4 * qs + a, ("A", a)))
        for j in range(4 * qs):
            units.append((NT_R + j, None))
        for a in range(4):
            units.append((NT_R + 4 * qs + a, ("B", a)))

        kevn = BK["p4_kev"]; qevn = BK["p4_qev"]; vevn = BK["p4_vev"]
        fz_prev = BK.get("p4_fz_prev", 0)
        g_exp = BK.setdefault("p4_g_exp", {})
        g_av = BK.setdefault("p4_g_av", {})
        gub = BK.get("p4_gu", 0)
        BK["p4_gu"] = gub + len(units)
        g_ptm = {}
        tms = {}
        BANKS6 = (5, 6, 0, 1, 3, 4)
        P = len(units) // 2

        def emit_s2(p):
            for dk in range(2):
                ui = 2 * p + dk
                gu = gub + ui
                kt, mk = units[ui]
                sbank = BANKS6[gu % 6]
                need_exp = g_exp.get(gu - 6, 0)
                ntm = s_t.inc()
                def f(t, kt=kt, sbank=sbank, need_exp=need_exp, kevn=kevn, qevn=qevn,
                      vevn=vevn, qs=qs, hh=hh, ui=ui):
                    if ui == 0:
                        t.wait_ge(s_v.h, max(kevn, qevn, vevn))
                    if need_exp:
                        t.wait_ge(s_s.h, need_exp)
                    t.matmul(PS[sbank][:, 0:512],
                             kpair[64 * hh:64 * hh + 64, kt * 128:(kt + 1) * 128],
                             qpair[64 * hh:64 * hh + 64, qs * 512:(qs + 1) * 512],
                             start=True, stop=True,
                             tile_position=(64 * hh, 0) if hh else None) \
                     .then_inc(s_t.h, 1)
                te(f)
                tms[ui] = ntm

        def emit_exp2(p):
            ui1 = 2 * p + 1
            gu1 = gub + ui1
            need_av = g_av.get(gu1 - NPT, 0)
            nse = s_s.inc(2)
            g_exp[gu1 - 1] = nse
            g_exp[gu1] = nse
            def f(s, ntm_=tms[ui1], need_av=need_av, gu1=gu1):
                s.wait_ge(s_t.h, ntm_)
                if need_av:
                    s.wait_ge(s_av.h, need_av)
                for gj in (gu1 - 1, gu1):
                    s.activation(out=PTS[gj % NPT], in_=PS[BANKS6[gj % 6]][:, 0:512],
                                 func=AF.Exp, bias=cexp[:, 0:1])
                s.drain().then_inc(s_s.h, 2)
            sa(f)
            mks = [units[2 * p][1], units[ui1][1]]
            assert (mks[0] is None) == (mks[1] is None)
            if mks[0] is not None:
                npm = s_v.inc()
                def f(v, nse=nse, p=p, mks=mks, gub=gub):
                    v.wait_ge(s_s.h, nse)
                    for dk in range(2):
                        ui = 2 * p + dk
                        kind, a = mks[dk]
                        ps_ = PTS[(gub + ui) % NPT]
                        if a:
                            v.memset(ps_[:, 0:128 * a], 0.0)
                        blk = ps_[:, 128 * a:128 * (a + 1)]
                        if kind == "A":
                            v.tensor_tensor(out=blk, in0=blk, in1=tril, op=OP.mult)
                        else:
                            v.tensor_scalar(out=blk, in0=blk, scalar1=cfl[:, 0:1],
                                            scalar2=None, op0=OP.mult)
                    v.drain().then_inc(s_v.h, 1)
                ve(f)
                g_ptm[2 * p] = ("v", npm)
                g_ptm[ui1] = ("v", npm)
            else:
                g_ptm[2 * p] = ("s", nse)
                g_ptm[ui1] = ("s", nse)

        def emit_rep2(p):
            for dk in range(2):
                ui = 2 * p + dk
                gu = gub + ui
                kt, mk = units[ui]
                nav = s_av.inc()
                g_av[gu] = nav
                eng, val = g_ptm[ui]
                last = (ui == len(units) - 1)
                def f(t, kt=kt, slot=gu % NPT, hh=hh, pe=pe, ui=ui, eng=eng, val=val,
                      last=last, fz_prev=fz_prev):
                    if ui == 0 and fz_prev:
                        t.wait_ge(s_fz.h, fz_prev)
                    if eng == "s":
                        t.wait_ge(s_s.h, val)
                    else:
                        t.wait_ge(s_v.h, val)
                    vt = v2[:, (hh * NT_E + kt) * 64:(hh * NT_E + kt) * 64 + 64]
                    pts = PTS[slot]
                    if pe == 0:
                        t.matmul(PS[7][0:64, 0:512], vt, pts, start=(ui == 0), stop=last)
                        t.matmul(PS[2][0:1, 0:512], onescol, pts, start=(ui == 0),
                                 stop=last).then_inc(s_av.h, 1)
                    else:
                        t.matmul(PS[7][64:128, 0:512], vt, pts, start=(ui == 0), stop=last,
                                 tile_position=(0, 64))
                        t.matmul(PS[2][64:65, 0:512], onescol, pts, start=(ui == 0),
                                 stop=last, tile_position=(0, 64)).then_inc(s_av.h, 1)
                te(f)

        emit_s2(0)
        for p in range(1, P):
            emit_exp2(p - 1)
            emit_s2(p)
            emit_rep2(p - 1)
        emit_exp2(P - 1)
        emit_rep2(P - 1)

        navl = s_av.n
        nfz1 = s_v.inc()
        def f(v, pe=pe, navl=navl, gmv=gmv, buf=buf, sgp=sgp):
            v.wait_ge(s_av.h, navl)
            v.wait_ge(sgp.h, gmv)
            v.reciprocal(lrow[pe:pe + 1, 0:512], PS[2][pe:pe + 1, 0:512])
            v.drain()
            v.tensor_tensor(out=glrow[pe:pe + 1, 0:512], in0=lrow[pe:pe + 1, 0:512],
                            in1=glstg[pe:pe + 1, buf * 512:(buf + 1) * 512],
                            op=OP.mult)
            v.drain().then_inc(s_v.h, 1)
        ve(f)
        ntr = s_t.inc()
        def f(t, pe=pe, nfz1=nfz1):
            t.wait_ge(s_v.h, nfz1)
            if pe == 0:
                t.matmul(PS[2][0:64, 0:512], ones_bf[0:1, 0:64], glrow[0:1, 0:512],
                         start=True, stop=True).then_inc(s_t.h, 1)
            else:
                t.matmul(PS[2][64:128, 0:512], ones_bf[64:65, 0:64],
                         glrow[64:65, 0:512], start=True, stop=True,
                         tile_position=(64, 64)).then_inc(s_t.h, 1)
        te(f)
        nfe = s_s.inc()
        def f(s, pe=pe, ntr=ntr):
            s.wait_ge(s_t.h, ntr)
            s.copy(fin[pe:pe + 64, 0:512], PS[2][pe:pe + 64, 0:512])
            s.drain().then_inc(s_s.h, 1)
        sa(f)
        nfz = s_fz.inc()
        BK["p4_fz_prev"] = nfz
        BK["p4_glstg_hist"].append(nfz)
        def f(v, pe=pe, hp=hp, qs=qs, nfe=nfe):
            v.wait_ge(s_s.h, nfe)
            v.tensor_tensor(out=gatedT[pe:pe + 64, hp * R + qs * 512:hp * R + (qs + 1) * 512],
                            in0=PS[7][pe:pe + 64, 0:512], in1=fin[pe:pe + 64, 0:512],
                            op=OP.mult)
            v.drain().then_inc(s_fz.h, 1)
        ve(f)

    def p5():
        stream_block.piece = 0
        stream_block.prev = [s_c.n, s_c.n]
        p4fz = BK["p4_fz"]
        first = True
        for ch in range(NDI):
            for (_, c0, cw) in DSPL:
                buf = stream_block.piece % 2
                st_ap = wld[:, buf * 512:buf * 512 + cw]
                need = stream_block.prev[buf]
                sw = s_wp[buf]
                ndma = sw.inc(16)
                def f(e, st_ap=st_ap, ch=ch, c0=c0, cw=cw, need=need, first=first,
                      p4fz=p4fz, sw=sw):
                    e.wait_ge(s_c.h, need)
                    if first:
                        e.wait_ge(s_fz.h, p4fz)
                    e.dma_start(st_ap, wout_in[ch * 128:ch * 128 + 128, c0:c0 + cw]) \
                     .then_inc(sw.h, 16)
                sy(f)
                ncst = s_c.inc()
                def f(v, st_ap=st_ap, ch=ch, c0=c0, cw=cw, ndma=ndma, sw=sw):
                    v.wait_ge(sw.h, ndma)
                    v.tensor_copy(wout_a[:, ch * D + c0:ch * D + c0 + cw], st_ap)
                    v.drain().then_inc(s_c.h, 1)
                ve(f)
                stream_block.prev[buf] = ncst
                stream_block.piece += 1
                first = False
        wor = s_c.n
        p4fzn = BK["p4_fz"]

        ep_done = {}
        for t in range(NT_R):
            buf = t % 2
            xt = xload[:, buf * D:(buf + 1) * D]
            need = ep_done.get(t - 2, 0)
            sx = s_xp[buf]
            ndma = sx.inc(16)
            def f(e, xt=xt, t=t, need=need, p4fzn=p4fzn, sx=sx):
                if t == 0:
                    e.wait_ge(s_fz.h, p4fzn)
                if need:
                    e.wait_ge(s_v.h, need)
                e.dma_start(xt, xe[t * 128:(t + 1) * 128, :]).then_inc(sx.h, 16)
            sy(f)

            ntm = s_t.inc()
            ep_prev = BK.get("p5_ep_prev", 0)
            def f(t_, t=t, wor=wor, p4fzn=p4fzn, ep_prev=ep_prev):
                if t == 0:
                    t_.wait_ge(s_c.h, wor)
                    t_.wait_ge(s_fz.h, p4fzn)
                if ep_prev:
                    t_.wait_ge(s_v.h, ep_prev)
                mm = None
                for pr in range(NPAIR):
                    for (bi, c0, cw) in DSPL:
                        mm = t_.matmul(PS[bi][:, 0:cw],
                                  gatedT[:, pr * R + t * 128:pr * R + (t + 1) * 128],
                                  wout_a[:, pr * D + c0:pr * D + c0 + cw],
                                  start=(pr == 0), stop=(pr == NPAIR - 1))
                mm.then_inc(s_t.h, 1)
            te(f)

            ot = otmp2.bitcast(F32)[:, buf * D:(buf + 1) * D]
            stp = s_stp[buf]
            st_prev = stp.n
            nep = s_v.inc()
            ep_done[t] = nep
            BK["p5_ep_prev"] = nep
            def f(v, t=t, xt=xt, ntm=ntm, ndma=ndma, ot=ot, stp=stp, st_prev=st_prev,
                  sx=sx):
                v.wait_ge(s_t.h, ntm)
                v.wait_ge(sx.h, ndma)
                if st_prev:
                    v.wait_ge(stp.h, st_prev)
                for (bi, c0, cw) in DSPL:
                    v.tensor_tensor(out=ot[:, c0:c0 + cw], in0=PS[bi][:, 0:cw],
                                    in1=g_out[:, t * D + c0:t * D + c0 + cw], op=OP.mult)
                v.tensor_tensor(out=ot, in0=ot, in1=xt, op=OP.add)
                v.drain().then_inc(s_v.h, 1)
            ve(f)

            nst = stp.inc(16)
            def f(e, t=t, nep=nep, ot=ot, stp=stp):
                e.wait_ge(s_v.h, nep)
                e.dma_start(out_d[t * 128:(t + 1) * 128, :], ot).then_inc(stp.h, 16)
            sy(f)

        nst0, nst1 = s_stp[0].n, s_stp[1].n
        nstf = s_st.n
        def f(e, nstf=nstf, nst0=nst0, nst1=nst1):
            if nstf:
                e.wait_ge(s_st.h, nstf)
            e.wait_ge(s_stp[0].h, nst0)
            e.wait_ge(s_stp[1].h, nst1)
        sy(f)

    nph = cfg.get("nphases", 6)
    with nc.Block() as b:
        blkref["b"] = b
        phases = [p0, p1, p2, p3, p4, p5]
        for ph in phases[:nph]:
            ph()

    nc.finalize()
    return nc


_CACHE = {}


def _perm_for(half, NT_R):
    own, other = [], []
    for i in range(NT_R):
        own.extend(range(256 * i + 128 * half, 256 * i + 128 * half + 128))
        other.extend(range(256 * i + 128 * (1 - half), 256 * i + 128 * (1 - half) + 128))
    return np.array(own + other, dtype=np.int64), np.array(own, dtype=np.int64)


def kernel(L0_attn_tokens, L0_attn_norm_w, L0_attn_Wq, L0_attn_Wkv, L0_attn_Wout,
           L0_attn_Wg, L0_ffn_Wq, L0_ffn_bq, L0_ffn_K1, L0_ffn_K2, L0_ffn_W_up,
           L0_ffn_W_down):
    cfg = full_cfg()
    NT_R = cfg["NT_R"]
    x = np.ascontiguousarray(np.asarray(L0_attn_tokens, dtype=np.float32))
    Bb, S, D = x.shape
    weights = dict(
        nw=np.ascontiguousarray(L0_attn_norm_w, dtype=np.float32),
        wq=np.ascontiguousarray(L0_attn_Wq, dtype=np.float32),
        wkv=np.ascontiguousarray(L0_attn_Wkv, dtype=np.float32),
        wout=np.ascontiguousarray(L0_attn_Wout, dtype=np.float32),
        wg=np.ascontiguousarray(L0_attn_Wg, dtype=np.float32),
        wf=np.ascontiguousarray(L0_ffn_Wq, dtype=np.float32),
        bq=np.ascontiguousarray(L0_ffn_bq, dtype=np.float32),
        k1=np.ascontiguousarray(L0_ffn_K1, dtype=np.float32),
        k2=np.ascontiguousarray(L0_ffn_K2, dtype=np.float32),
        wdn=np.ascontiguousarray(L0_ffn_W_down, dtype=np.float32),
        wup=np.ascontiguousarray(L0_ffn_W_up, dtype=np.float32),
    )
    if "nc" not in _CACHE:
        _CACHE["nc"] = build(cfg)
    nc = _CACHE["nc"]

    in_maps = []
    perms = []
    for c in range(8):
        b, half = c // 2, c % 2
        perm, own = _perm_for(half, NT_R)
        perms.append(own)
        m = dict(weights)
        m["xe"] = np.ascontiguousarray(x[b][perm])
        m["cflag"] = np.full((128, 1), float(half), dtype=np.float32)
        in_maps.append(m)

    res = run_bass_kernel_spmd(nc, in_maps, core_ids=list(range(8)))
    out = np.empty((Bb, S, D), dtype=np.float32)
    for c in range(8):
        b = c // 2
        out[b][perms[c]] = res.results[c]["out"]
    return out
